# version 10
# baseline (speedup 1.0000x reference)
import contextlib
import numpy as np
import ml_dtypes
import concourse.bass as bass
import concourse.mybir as mybir
from concourse.bass_utils import run_bass_kernel_spmd

F32 = mybir.dt.float32
BF16 = mybir.dt.bfloat16
U8 = mybir.dt.uint8
AF = mybir.ActivationFunctionType
ALU = mybir.AluOpType
AX = mybir.AxisListType
BF = ml_dtypes.bfloat16

T, D = 2048, 1024
NT, ND = 16, 8
H, DH = 16, 64
E = 8
F0 = 4096
NFC = F0 // 128
LNEPS = 1e-5
A0EPS = 1e-6
G1EPS = 1e-4
WTHR = 0.1
GC0 = 0.044715
GC1 = 0.7978845608028654
ARENA = 207 * 1024


ENGINES = ("sync", "gpsimd", "vector", "scalar", "tensor")


NSLOT = 8


class _Step:
    __slots__ = ("eng", "meth", "kw", "inc", "cnt", "deps", "bar", "skey")


class SP:
    def __init__(self, serial=False):
        self.steps = []
        self.nq = {}
        self.last_w = {}
        self.readers = {}
        self.bar_last = {}
        self.last_of = {}
        self.last_slot = {}
        self.serial = serial

    def barrier(self):
        self.bar_last = dict(self.last_of)
        self.last_w = {}
        self.readers = {}

    def on(self, ek, meth, r=(), w=(), **kw):
        st = _Step()
        st.eng, st.meth, st.kw = ek, meth, kw
        st.deps = set()
        if meth == "dma_start":
            st.inc = 16
            n = self.nq.get(ek, 0)
            self.nq[ek] = n + 1
            st.skey = f"{ek}:{n % NSLOT}"
            prev = self.last_slot.get(st.skey)
            if prev is not None:
                st.deps.add(prev)
            self.last_slot[st.skey] = st
        else:
            st.inc = 1
            st.skey = ek
        st.deps.update(self.bar_last.values())
        for tok in r:
            lw = self.last_w.get(tok)
            if lw is not None:
                st.deps.add(lw)
            self.readers.setdefault(tok, []).append(st)
        for tok in w:
            lw = self.last_w.get(tok)
            if lw is not None:
                st.deps.add(lw)
            for rd in self.readers.get(tok, ()):
                st.deps.add(rd)
            self.readers[tok] = []
            self.last_w[tok] = st
        if self.serial and self.steps:
            st.deps.add(self.steps[-1])
        st.deps.discard(st)
        self.last_of[st.skey] = st
        self.steps.append(st)

    def dma(self, q, out, in_, r=(), w=(), **kw):
        self.on(q, "dma_start", r=r, w=w, out=out, in_=in_, **kw)


def emit(nc, sp):
    needed = set(sp.last_of.values())
    for st in sp.steps:
        needed.update(st.deps)
    cnt = {}
    for st in sp.steps:
        if st in needed:
            cnt[st.skey] = cnt.get(st.skey, 0) + st.inc
            st.cnt = cnt[st.skey]
        else:
            st.cnt = 0
    skeys = sorted(cnt)
    for k in skeys:
        assert cnt[k] < 65000, (k, cnt[k])
    es = contextlib.ExitStack()
    with es:
        sems = {k: es.enter_context(nc.semaphore("s_" + k.replace(":", "_")))
                for k in skeys}
        blk = es.enter_context(nc.Block("main"))

        def make(ek, final=False):
            def run(eng):
                waited = {k: 0 for k in skeys}
                for st in sp.steps:
                    if st.eng != ek:
                        continue
                    need = {}
                    for d in st.deps:
                        assert d.cnt > 0
                        if d.cnt > need.get(d.skey, 0):
                            need[d.skey] = d.cnt
                    for k2, v in sorted(need.items()):
                        if v > waited[k2]:
                            eng.wait_ge(sems[k2], v)
                            waited[k2] = v
                    instr = getattr(eng, st.meth)(**st.kw)
                    if st in needed:
                        instr.then_inc(sems[st.skey], st.inc)
                if final:
                    for k2 in skeys:
                        if cnt[k2] > waited[k2]:
                            eng.wait_ge(sems[k2], cnt[k2])
            return run
        blk.sync(make("sync", final=True))
        blk.gpsimd(make("gpsimd"))
        blk.vector(make("vector"))
        blk.scalar(make("scalar"))
        blk.tensor(make("tensor"))


class Arena:
    def __init__(self, big):
        self.big = big
        self.off = 0

    def t(self, shape, dtype, p0=0):
        n = int(np.prod(shape[1:]))
        nbytes = n * (2 if dtype == BF16 else 4)
        v = self.big[p0:p0 + shape[0], self.off:self.off + nbytes].bitcast(dtype)
        if len(shape) == 3:
            v = v.rearrange("p (a b) -> p a b", b=shape[2])
        elif len(shape) == 4:
            v = v.rearrange("p (a b c) -> p a b c", b=shape[2], c=shape[3])
        self.off += (nbytes + 127) & ~127
        assert self.off <= ARENA, f"arena overflow: {self.off}"
        return v


def _bass():
    return bass.Bass("TRN2", target_bir_lowering=False)


class Pool:
    def __init__(self, nc, es):
        self.PA = es.enter_context(nc.psum_tensor("PA", [128, 1024], F32))
        self.PB = es.enter_context(nc.psum_tensor("PB", [128, 1024], F32))
        self.PC = es.enter_context(nc.psum_tensor("PC", [128, 1024], F32))
        self.PT = es.enter_context(nc.psum_tensor("PT", [128, 2048], BF16))


def phase_ln(nc, sp, ar, ps, io, residual):
    x_h, gw_h, gb_h, idb_h = io["x"], io["gw"], io["gb"], io["idb"]
    xnf_h, gl_h = io["xnf"], io["glog"]
    X32 = ar.t([128, D], F32)
    A32 = ar.t([128, D], F32)
    B32 = ar.t([128, D], F32)
    SQ = ar.t([128, D], F32)
    st = ar.t([128, 32], F32)
    xh = ar.t([128, D], BF16)
    xl = ar.t([128, D], BF16)
    XNH = ar.t([128, D], BF16)
    XNL = ar.t([128, D], BF16)
    sw32 = ar.t([128, ND, E], F32)
    swh = ar.t([128, ND, E], BF16)
    swl = ar.t([128, ND, E], BF16)
    gbrow = ar.t([1, E], F32)
    ones1 = ar.t([1, 128], F32)
    idb = ar.t([128, 128], BF16)
    gl32 = ar.t([128, E], F32)
    psT, psG = ps.PT, ps.PC[:, 0:E]

    gw_v = gw_h.rearrange("(c p) e -> p c e", p=128)
    sp.on("vector", "memset", w=["ones1"], ap=ones1[:, :], constant=1.0)
    sp.dma("sync", out=idb[:, :], in_=idb_h, w=["idb"])
    sp.dma("sync", out=gbrow[:, :], in_=gb_h, w=["gbrow"])
    for dc in range(ND):
        sp.dma("sync", out=sw32[:, dc, :], in_=gw_v[:, dc, :], w=["sw32"])
    sp.on("vector", "tensor_copy", r=["sw32"], w=["swh"],
          out=swh[:, :, :], in_=sw32[:, :, :])
    sp.on("vector", "tensor_tensor", r=["sw32", "swh"], w=["swl"],
          out=swl[:, :, :], in0=sw32[:, :, :], in1=swh[:, :, :],
          op=ALU.subtract)

    for tt in range(NT):
        r = slice(tt * 128, (tt + 1) * 128)
        sp.dma("sync", out=X32[:, :], in_=x_h[r, :], w=["X32"])
        if residual:
            sp.dma("sync", out=A32[:, :], in_=io["a0"][r, :], w=["A32"])
            sp.dma("sync", out=B32[:, :], in_=io["f0"][r, :], w=["B32"])
            sp.on("vector", "tensor_tensor", r=["X32", "A32"], w=["X32"],
                  out=X32[:, :], in0=X32[:, :], in1=A32[:, :], op=ALU.add)
            sp.on("vector", "tensor_tensor", r=["X32", "B32"], w=["X32"],
                  out=X32[:, :], in0=X32[:, :], in1=B32[:, :], op=ALU.add)
            sp.dma("gpsimd", out=io["x1"][r, :], in_=X32[:, :], r=["X32"])
        sp.on("vector", "reduce_sum", r=["X32"], w=["st"],
              out=st[:, 0:1], in_=X32[:, :], axis=AX.X)
        sp.on("vector", "tensor_scalar", r=["st"], w=["st"],
              out=st[:, 1:2], in0=st[:, 0:1],
              scalar1=1.0 / D, scalar2=None, op0=ALU.mult)
        sp.on("scalar", "activation", r=["X32"], w=["SQ", "st"],
              out=SQ[:, :], in_=X32[:, :], func=AF.Square, accum_out=st[:, 2:3])
        sp.on("vector", "tensor_scalar", r=["st"], w=["st"],
              out=st[:, 3:4], in0=st[:, 2:3],
              scalar1=1.0 / D, scalar2=None, op0=ALU.mult)
        sp.on("vector", "scalar_tensor_tensor", r=["st"], w=["st"],
              out=st[:, 4:5], in0=st[:, 1:2],
              scalar=st[:, 1:2], in1=st[:, 3:4], op0=ALU.mult, op1=ALU.subtract)
        sp.on("vector", "tensor_scalar", r=["st"], w=["st"],
              out=st[:, 5:6], in0=st[:, 4:5],
              scalar1=-1.0, scalar2=LNEPS, op0=ALU.mult, op1=ALU.add)
        sp.on("scalar", "activation", r=["st"], w=["st"],
              out=st[:, 6:7], in_=st[:, 5:6], func=AF.Sqrt)
        sp.on("vector", "reciprocal", r=["st"], w=["st"],
              out=st[:, 7:8], in_=st[:, 6:7])
        sp.on("vector", "tensor_tensor", r=["st"], w=["st"],
              out=st[:, 8:9], in0=st[:, 1:2], in1=st[:, 7:8], op=ALU.mult)
        sp.on("vector", "tensor_scalar", r=["X32", "st"], w=["SQ"],
              out=SQ[:, :], in0=X32[:, :],
              scalar1=st[:, 7:8], scalar2=st[:, 8:9], op0=ALU.mult,
              op1=ALU.subtract)
        sp.on("vector", "tensor_copy", r=["SQ"], w=["xh"],
              out=xh[:, :], in_=SQ[:, :])
        sp.on("vector", "tensor_tensor", r=["SQ", "xh"], w=["xl"],
              out=xl[:, :], in0=SQ[:, :], in1=xh[:, :], op=ALU.subtract)
        for dc in range(ND):
            c = slice(dc * 128, (dc + 1) * 128)
            sp.on("tensor", "transpose", r=["xh", "idb"], w=["psT"],
                  out=psT[:, c], in_=xh[:, c], identity=idb[:, :])
        for dc in range(ND):
            sp.on("tensor", "transpose", r=["xl", "idb"], w=["psT"],
                  out=psT[:, 1024 + dc * 128:1024 + (dc + 1) * 128],
                  in_=xl[:, dc * 128:(dc + 1) * 128], identity=idb[:, :])
        sp.on("vector", "tensor_copy", r=["psT"], w=["XNH"],
              out=XNH[:, :], in_=psT[:, 0:1024])
        sp.on("vector", "tensor_copy", r=["psT"], w=["XNL"],
              out=XNL[:, :], in_=psT[:, 1024:2048])
        XNHv = XNH[:, :].rearrange("p (c t) -> p c t", t=128)
        XNLv = XNL[:, :].rearrange("p (c t) -> p c t", t=128)
        sp.on("tensor", "matmul", r=["ones1", "gbrow"], w=["psG"],
              out=psG[:, :], lhsT=ones1[:, :], rhs=gbrow[:, :],
              start=True, stop=False)
        for dc in range(ND):
            last = dc == ND - 1
            sp.on("tensor", "matmul", r=["XNH", "swh"], w=["psG"],
                  out=psG[:, :], lhsT=XNHv[:, dc, :], rhs=swh[:, dc, :],
                  start=False, stop=False)
            sp.on("tensor", "matmul", r=["XNH", "swl"], w=["psG"],
                  out=psG[:, :], lhsT=XNHv[:, dc, :], rhs=swl[:, dc, :],
                  start=False, stop=False)
            sp.on("tensor", "matmul", r=["XNL", "swh"], w=["psG"],
                  out=psG[:, :], lhsT=XNLv[:, dc, :], rhs=swh[:, dc, :],
                  start=False, stop=last)
        sp.on("vector", "tensor_copy", r=["psG"], w=["gl32"],
              out=gl32[:, :], in_=psG[:, :])
        sp.dma("gpsimd", out=gl_h[r, :], in_=gl32[:, :], r=["gl32"])
        for dc in range(ND):
            sp.dma("gpsimd", out=xnf_h[dc, :, r],
                   in_=XNH[:, dc * 128:(dc + 1) * 128], r=["XNH"])


def phase_attn0(nc, sp, ar, ps, io):
    xnf_h, a0_h, idb_h = io["xnf"], io["a0"], io["idb"]
    xns = ar.t([128, ND, T], BF16)
    wq_s = ar.t([128, ND, D], BF16)
    wk_s = ar.t([128, ND, D], BF16)
    wv_s = ar.t([128, ND, D], BF16)
    wo_s = ar.t([128, ND, D], BF16)
    brow = ar.t([1, 4 * D], F32)
    ones1 = ar.t([1, 128], F32)
    onec = ar.t([1, 1], F32)
    idb = ar.t([128, 128], BF16)
    BK32 = ar.t([128, D], F32)
    BV32 = ar.t([128, D], F32)
    BO32 = ar.t([128, D], F32)
    bqF = ar.t([128, ND], F32)
    SA = ar.t([128, D], F32)
    SB = ar.t([128, D], F32)
    SC = ar.t([128, D], F32)
    Ktm = ar.t([128, D], BF16)
    Vext = ar.t([128, H, 66], BF16)
    KVb = ar.t([128, 520], BF16)
    Qf = ar.t([128, D], BF16)
    nrm = ar.t([128, 8], F32)
    rn = ar.t([128, 8], F32)
    AO32 = ar.t([128, D], F32)
    AObf = ar.t([128, D], BF16)
    Af = ar.t([128, D], BF16)
    OUT32 = ar.t([128, D], F32)
    psA, psV, psKV = ps.PA, ps.PB, ps.PC
    psTa = ps.PT[:, 0:1024]

    sp.on("vector", "memset", w=["ones1"], ap=ones1[:, :], constant=1.0)
    sp.on("vector", "memset", w=["onec"], ap=onec[:, :], constant=1.0)
    sp.on("vector", "memset", w=["Vext"],
          ap=Vext[:, :, :].rearrange("p a b -> p (a b)"), constant=1.0)
    sp.dma("sync", out=idb[:, :], in_=idb_h, w=["idb"])
    for off, srcb in ((0, io["bq"]), (D, io["bk"]), (2 * D, io["bv"]),
                      (3 * D, io["bo"])):
        sp.dma("sync", out=brow[:, off:off + D], in_=srcb, w=["brow"])
    for dc in range(ND):
        c = slice(dc * 128, (dc + 1) * 128)
        sp.dma("sync", out=xns[:, dc, :], in_=xnf_h[dc, :, :],
               w=[f"xns{dc}"])
        sp.dma("gpsimd", out=wq_s[:, dc, :], in_=io["wqT"][c, :],
               w=[f"wq{dc}"])
        sp.dma("gpsimd", out=wk_s[:, dc, :], in_=io["wkT"][c, :],
               w=[f"wk{dc}"])
        sp.dma("gpsimd", out=wv_s[:, dc, :], in_=io["wvT"][c, :],
               w=[f"wv{dc}"])
        sp.dma("gpsimd", out=wo_s[:, dc, :], in_=io["woT"][c, :],
               w=[f"wo{dc}"])
    for dst, tok, off in ((BK32, "BK32", D), (BV32, "BV32", 2 * D),
                          (BO32, "BO32", 3 * D)):
        for half in range(2):
            hc = slice(half * 512, (half + 1) * 512)
            sp.on("tensor", "matmul", r=["ones1", "brow"], w=["psA"],
                  out=psA[:, hc], lhsT=ones1[:, :],
                  rhs=brow[:, off + half * 512:off + (half + 1) * 512],
                  start=True, stop=True)
        sp.on("vector", "tensor_copy", r=["psA"], w=[tok],
              out=dst[:, :], in_=psA[:, :])
    for dc in range(ND):
        sp.on("tensor", "matmul", r=["brow", "onec"], w=["psKV"],
              out=psKV[:, dc:dc + 1],
              lhsT=brow[:, dc * 128:(dc + 1) * 128], rhs=onec[:, :],
              start=True, stop=True)
    sp.on("vector", "tensor_copy", r=["psKV"], w=["bqF"],
          out=bqF[:, :], in_=psKV[:, 0:ND])

    for tt in range(NT):
        for half in range(2):
            hc = slice(half * 512, (half + 1) * 512)
            for dc in range(ND):
                sp.on("tensor", "matmul", r=[f"xns{dc}", f"wk{dc}"], w=["psA"],
                      out=psA[:, hc],
                      lhsT=xns[:, dc, tt * 128:(tt + 1) * 128],
                      rhs=wk_s[:, dc, hc],
                      start=(dc == 0), stop=(dc == ND - 1))
            for dc in range(ND):
                sp.on("tensor", "matmul", r=[f"xns{dc}", f"wv{dc}"], w=["psV"],
                      out=psV[:, hc],
                      lhsT=xns[:, dc, tt * 128:(tt + 1) * 128],
                      rhs=wv_s[:, dc, hc],
                      start=(dc == 0), stop=(dc == ND - 1))
        sp.on("vector", "tensor_tensor", r=["psA", "BK32"], w=["SA"],
              out=SA[:, :], in0=psA[:, :], in1=BK32[:, :], op=ALU.add)
        sp.on("vector", "tensor_scalar", r=["SA"], w=["SB"],
              out=SB[:, :], in0=SA[:, :],
              scalar1=0.0, scalar2=None, op0=ALU.max)
        sp.on("vector", "tensor_scalar", r=["SA"], w=["SC"],
              out=SC[:, :], in0=SA[:, :],
              scalar1=0.0, scalar2=None, op0=ALU.min)
        sp.on("scalar", "activation", r=["SC"], w=["SC"],
              out=SC[:, :], in_=SC[:, :], func=AF.Exp)
        sp.on("vector", "tensor_tensor", r=["SB", "SC"], w=["Ktm"],
              out=Ktm[:, :], in0=SB[:, :], in1=SC[:, :], op=ALU.add)
        sp.on("vector", "tensor_tensor", r=["psV", "BV32", "Vext"], w=["Vext"],
              out=Vext[:, :, 0:64],
              in0=psV[:, :].rearrange("p (h d) -> p h d", d=64),
              in1=BV32[:, :].rearrange("p (h d) -> p h d", d=64), op=ALU.add)
        for h in range(H):
            po = (h % 2) * 64
            pr = h // 2
            ko = (pr % 4) * 65 + (pr // 4) * 512
            sp.on("tensor", "matmul", r=["Ktm", "Vext"], w=["psKV"],
                  out=psKV[po:po + 64, ko:ko + 65],
                  lhsT=Ktm[:, h * 64:(h + 1) * 64], rhs=Vext[:, h, 0:65],
                  start=(tt == 0), stop=(tt == NT - 1), skip_group_check=True)
    sp.on("vector", "tensor_copy", r=["psKV"], w=["KVb"],
          out=KVb[:, 0:260], in_=psKV[:, 0:260])
    sp.on("vector", "tensor_copy", r=["psKV"], w=["KVb"],
          out=KVb[:, 260:520], in_=psKV[:, 512:772])

    psO = psKV

    def kvb_off(pr):
        return (pr % 4) * 65 + (pr // 4) * 260

    psOv0 = psO[:, 0:260].rearrange("p (h c) -> p h c", c=65)
    psOv1 = psO[:, 512:772].rearrange("p (h c) -> p h c", c=65)
    for tt in range(NT):
        tr = slice(tt * 128, (tt + 1) * 128)
        for pr in range(ND):
            for dc in range(ND):
                sp.on("tensor", "matmul", r=[f"wq{dc}", f"xns{dc}"], w=["psA"],
                      out=psA[:, pr * 128:(pr + 1) * 128],
                      lhsT=wq_s[:, dc, pr * 128:(pr + 1) * 128],
                      rhs=xns[:, dc, tr],
                      start=(dc == 0), stop=(dc == ND - 1))
        for pr in range(ND):
            c = slice(pr * 128, (pr + 1) * 128)
            sp.on("vector", "tensor_scalar", r=["psA", "bqF"], w=["SB"],
                  out=SB[:, c], in0=psA[:, c],
                  scalar1=bqF[:, pr:pr + 1], scalar2=0.0, op0=ALU.add,
                  op1=ALU.max)
            sp.on("vector", "tensor_scalar", r=["psA", "bqF"], w=["SC"],
                  out=SC[:, c], in0=psA[:, c],
                  scalar1=bqF[:, pr:pr + 1], scalar2=0.0, op0=ALU.add,
                  op1=ALU.min)
        sp.on("scalar", "activation", r=["SC"], w=["SC"],
              out=SC[:, :], in_=SC[:, :], func=AF.Exp)
        sp.on("vector", "tensor_tensor", r=["SB", "SC"], w=["Qf"],
              out=Qf[:, :], in0=SB[:, :], in1=SC[:, :], op=ALU.add)
        for g8 in range(2):
            for hh in range(8):
                h = g8 * 8 + hh
                po = (h % 2) * 64
                oo = (hh % 4) * 65 + (hh // 4) * 512
                sp.on("tensor", "matmul", r=["Qf", "KVb"], w=["psKV"],
                      out=psO[:, oo:oo + 65],
                      lhsT=Qf[po:po + 64, (h // 2) * 128:(h // 2) * 128 + 128],
                      rhs=KVb[po:po + 64, kvb_off(h // 2):kvb_off(h // 2) + 65],
                      start=True, stop=True, skip_group_check=True)
            sp.on("vector", "tensor_scalar", r=["psKV"], w=["nrm"],
                  out=nrm[:, 0:4], in0=psOv0[:, :, 64:65],
                  scalar1=A0EPS, scalar2=None, op0=ALU.add)
            sp.on("vector", "tensor_scalar", r=["psKV"], w=["nrm"],
                  out=nrm[:, 4:8], in0=psOv1[:, :, 64:65],
                  scalar1=A0EPS, scalar2=None, op0=ALU.add)
            sp.on("vector", "reciprocal", r=["nrm"], w=["rn"],
                  out=rn[:, :], in_=nrm[:, :])
            for hh in range(8):
                h = g8 * 8 + hh
                pv = psOv0 if hh < 4 else psOv1
                sp.on("vector", "tensor_scalar", r=["psKV", "rn"], w=["AO32"],
                      out=AO32[:, h * 64:(h + 1) * 64],
                      in0=pv[:, hh % 4, 0:64], scalar1=rn[:, hh:hh + 1],
                      scalar2=None, op0=ALU.mult)
        sp.on("vector", "tensor_copy", r=["AO32"], w=["AObf"],
              out=AObf[:, :], in_=AO32[:, :])
        for dc in range(ND):
            c = slice(dc * 128, (dc + 1) * 128)
            sp.on("tensor", "transpose", r=["AObf", "idb"], w=["psTa"],
                  out=psTa[:, c], in_=AObf[:, c], identity=idb[:, :])
        sp.on("vector", "tensor_copy", r=["psTa"], w=["Af"],
              out=Af[:, :], in_=psTa[:, :])
        for half in range(2):
            hc = slice(half * 512, (half + 1) * 512)
            for dc in range(ND):
                sp.on("tensor", "matmul", r=["Af", f"wo{dc}"], w=["psV"],
                      out=psV[:, hc],
                      lhsT=Af[:, dc * 128:(dc + 1) * 128],
                      rhs=wo_s[:, dc, hc],
                      start=(dc == 0), stop=(dc == ND - 1))
        sp.on("vector", "tensor_tensor", r=["psV", "BO32"], w=["OUT32"],
              out=OUT32[:, :], in0=psV[:, :], in1=BO32[:, :], op=ALU.add)
        sp.dma("gpsimd", out=a0_h[tr, :], in_=OUT32[:, :], r=["OUT32"])


def phase_ffn0(nc, sp, ar, ps, io):
    xnf_h, gl_h, f0_h = io["xnf"], io["glog"], io["f0"]
    w1_h, b1_h, w2_h, b2_h = io["w1"], io["b1"], io["w2"], io["b2"]
    w1_s = ar.t([128, ND, F0], BF16)
    w2_s = ar.t([128, NFC, D], BF16)
    xnt = ar.t([128, ND, 512], BF16)
    hbf = ar.t([128, NFC, 512], BF16)
    b1row = ar.t([1, D], F32)
    b2row = ar.t([1, D], F32)
    b1col = ar.t([128, NFC], F32)
    B2B = ar.t([128, D], F32)
    ones1 = ar.t([1, 128], F32)
    onec = ar.t([1, 1], F32)
    l8 = ar.t([128, E], F32)
    SEL = ar.t([128, NT, E], F32)
    RPM = ar.t([128, NT], F32)
    st = ar.t([128, 4], F32)
    GAs = (ar.t([128, 512], F32), ar.t([128, 512], F32))
    GBs = (ar.t([128, 512], F32), ar.t([128, 512], F32))
    Y32 = ar.t([128, D], F32)
    OUT32 = ar.t([128, D], F32)
    psY, psC = ps.PB, ps.PC[:, 0:NFC]
    psHs = (ps.PA[:, 0:512], ps.PA[:, 512:1024])

    sp.on("vector", "memset", w=["ones1"], ap=ones1[:, :], constant=1.0)
    sp.on("vector", "memset", w=["onec"], ap=onec[:, :], constant=1.0)
    for tt in range(NT):
        r = slice(tt * 128, (tt + 1) * 128)
        sp.dma("sync", out=l8[:, :], in_=gl_h[r, :], w=["l8"])
        sp.on("vector", "reduce_max", r=["l8"], w=["st"],
              out=st[:, 0:1], in_=l8[:, :], axis=AX.X, negate=True)
        sp.on("scalar", "activation", r=["l8", "st"], w=["SEL", "st"],
              out=SEL[:, tt, :], in_=l8[:, :],
              func=AF.Exp, bias=st[:, 0:1], accum_out=st[:, 1:2])
        sp.on("vector", "reciprocal", r=["st"], w=["RPM"],
              out=RPM[:, tt:tt + 1], in_=st[:, 1:2])
        sp.on("vector", "tensor_scalar", r=["l8", "st"], w=["SEL"],
              out=SEL[:, tt, :], in0=l8[:, :],
              scalar1=st[:, 0:1], scalar2=0.0, op0=ALU.add, op1=ALU.is_equal)
    for e in range(E):
        w1v = w1_h[e].rearrange("(c p) f -> p c f", p=128)
        w2v = w2_h[e].rearrange("(c p) f -> p c f", p=128)
        for dc in range(ND):
            sp.dma("gpsimd", out=w1_s[:, dc, :], in_=w1v[:, dc, :],
                   w=[f"w1_{dc}"])
        for fc in range(NFC):
            sp.dma("gpsimd", out=w2_s[:, fc, :], in_=w2v[:, fc, :],
                   w=[f"w2_{fc}"])
        sp.dma("sync", out=b2row[:, :], in_=b2_h[e:e + 1, :], w=["b2row"])
        for ch in range(4):
            sp.dma("sync", out=b1row[:, :],
                   in_=b1_h[e:e + 1, ch * D:(ch + 1) * D], w=["b1row"])
            for j in range(8):
                sp.on("tensor", "matmul", r=["b1row", "onec"], w=["psC"],
                      out=psC[:, ch * 8 + j:ch * 8 + j + 1],
                      lhsT=b1row[:, j * 128:(j + 1) * 128], rhs=onec[:, :],
                      start=True, stop=True)
        sp.on("vector", "tensor_copy", r=["psC"], w=["b1col"],
              out=b1col[:, :], in_=psC[:, :])
        for half in range(2):
            sp.on("tensor", "matmul", r=["ones1", "b2row"], w=["psY"],
                  out=psY[:, half * 512:(half + 1) * 512],
                  lhsT=ones1[:, :], rhs=b2row[:, half * 512:(half + 1) * 512],
                  start=True, stop=True)
        sp.on("vector", "tensor_copy", r=["psY"], w=["B2B"],
              out=B2B[:, :], in_=psY[:, :])
        for sl in range(4):
            sr = slice(sl * 512, (sl + 1) * 512)
            for dc in range(ND):
                sp.dma("sync", out=xnt[:, dc, :], in_=xnf_h[dc, :, sr],
                       w=[f"xnt{dc}"])
            for fc in range(NFC):
                par = fc % 2
                psH = psHs[par]
                htok = f"psH{par}"
                GA, GB = GAs[par], GBs[par]
                ga, gb = f"GA{par}", f"GB{par}"
                for dc in range(ND):
                    sp.on("tensor", "matmul",
                          r=[f"w1_{dc}", f"xnt{dc}"], w=[htok],
                          out=psH[:, :],
                          lhsT=w1_s[:, dc, fc * 128:(fc + 1) * 128],
                          rhs=xnt[:, dc, :],
                          start=(dc == 0), stop=(dc == ND - 1))
                sp.on("vector", "tensor_scalar", r=[htok, "b1col"], w=[ga],
                      out=GA[:, :], in0=psH[:, :],
                      scalar1=b1col[:, fc:fc + 1], scalar2=None, op0=ALU.add)
                sp.on("gpsimd", "tensor_tensor", r=[ga], w=[gb],
                      out=GB[:, :], in0=GA[:, :], in1=GA[:, :], op=ALU.mult)
                sp.on("vector", "tensor_scalar", r=[gb], w=[gb],
                      out=GB[:, :], in0=GB[:, :],
                      scalar1=GC0, scalar2=1.0, op0=ALU.mult, op1=ALU.add)
                sp.on("gpsimd", "tensor_tensor", r=[gb, ga], w=[gb],
                      out=GB[:, :], in0=GB[:, :], in1=GA[:, :], op=ALU.mult)
                sp.on("scalar", "activation", r=[gb], w=[gb],
                      out=GB[:, :], in_=GB[:, :], func=AF.Tanh, scale=GC1)
                sp.on("vector", "scalar_tensor_tensor", r=[gb, ga], w=[gb],
                      out=GB[:, :], in0=GB[:, :], scalar=1.0, in1=GA[:, :],
                      op0=ALU.add, op1=ALU.mult)
                sp.on("scalar", "activation", r=[gb], w=["hbf"],
                      out=hbf[:, fc, :], in_=GB[:, :], func=AF.Copy,
                      scale=0.5)
            for q in range(4):
                tt = sl * 4 + q
                for half in range(2):
                    hc = slice(half * 512, (half + 1) * 512)
                    for fc in range(NFC):
                        sp.on("tensor", "matmul",
                              r=["hbf", f"w2_{fc}"], w=["psY"],
                              out=psY[:, hc],
                              lhsT=hbf[:, fc, q * 128:(q + 1) * 128],
                              rhs=w2_s[:, fc, hc],
                              start=(fc == 0), stop=(fc == NFC - 1))
                sp.on("vector", "tensor_tensor", r=["psY", "B2B"], w=["Y32"],
                      out=Y32[:, :], in0=psY[:, :], in1=B2B[:, :], op=ALU.add)
                tr = slice(tt * 128, (tt + 1) * 128)
                if e == 0:
                    sp.on("vector", "tensor_scalar",
                          r=["Y32", "SEL"], w=["OUT32"],
                          out=OUT32[:, :], in0=Y32[:, :],
                          scalar1=SEL[:, tt, e:e + 1],
                          scalar2=None, op0=ALU.mult)
                else:
                    sp.dma("sync", out=OUT32[:, :], in_=f0_h[tr, :],
                           r=[f"f0_{tt}"], w=["OUT32"])
                    sp.on("vector", "scalar_tensor_tensor",
                          r=["Y32", "SEL", "OUT32"], w=["OUT32"],
                          out=OUT32[:, :], in0=Y32[:, :],
                          scalar=SEL[:, tt, e:e + 1],
                          in1=OUT32[:, :], op0=ALU.mult, op1=ALU.add)
                sp.dma("gpsimd", out=f0_h[tr, :], in_=OUT32[:, :],
                       r=["OUT32"], w=[f"f0_{tt}"])
    for tt in range(NT):
        r = slice(tt * 128, (tt + 1) * 128)
        sp.dma("sync", out=OUT32[:, :], in_=f0_h[r, :],
               r=[f"f0_{tt}"], w=["OUT32"])
        sp.on("vector", "tensor_scalar", r=["OUT32", "RPM"], w=["Y32"],
              out=Y32[:, :], in0=OUT32[:, :],
              scalar1=RPM[:, tt:tt + 1], scalar2=None, op0=ALU.mult)
        sp.dma("gpsimd", out=f0_h[r, :], in_=Y32[:, :],
               r=["Y32"], w=[f"f0_{tt}"])


def phase_attn1(nc, sp, ar, ps, io):
    xnf_h, a1_h = io["xnf"], io["a1"]
    xns = ar.t([128, ND, T], BF16)
    kf = ar.t([128, ND, T], BF16)
    vg = ar.t([128, NT, D], BF16)
    wp_s = ar.t([128, ND, D], BF16)
    wo_s = ar.t([128, ND, D], BF16)
    brow = ar.t([1, 3 * D], F32)
    borow = ar.t([1, D], F32)
    ones1 = ar.t([1, 128], F32)
    idb = ar.t([128, 128], BF16)
    cosd = ar.t([128, NT, 64], F32)
    sind = ar.t([128, NT, 64], F32)
    tri = ar.t([128, 128], F32)
    BB32 = ar.t([128, D], F32)
    BO32 = ar.t([128, D], F32)
    Q32 = ar.t([128, D], F32)
    QS32 = ar.t([128, D], F32)
    R1 = ar.t([128, D], F32)
    QB = ar.t([128, D], BF16)
    qft = ar.t([128, D], BF16)
    S32 = ar.t([128, T], F32)
    Pbf = ar.t([128, T], BF16)
    PTb = ar.t([128, 1024], BF16)
    st = ar.t([128, 4], F32)
    AO32 = ar.t([128, D], F32)
    AObf = ar.t([128, D], BF16)
    Af = ar.t([128, D], BF16)
    OUT32 = ar.t([128, D], F32)
    psK, psY = ps.PA, ps.PB
    psT = ps.PT[:, 0:1024]
    psS = ps.PA[:, 0:512]
    psO = ps.PC[:, 0:64]

    sp.on("vector", "memset", w=["ones1"], ap=ones1[:, :], constant=1.0)
    sp.dma("sync", out=idb[:, :], in_=io["idb"], w=["idb"])
    sp.dma("sync", out=tri[:, :], in_=io["tri"], w=["tri"])
    sp.dma("sync", out=cosd[:, :, :], in_=io["cosd"], w=["cosd"])
    sp.dma("sync", out=sind[:, :, :], in_=io["sind"], w=["sind"])
    sp.dma("sync", out=brow[:, :], in_=io["bqkv"], w=["brow"])
    sp.dma("sync", out=borow[:, :], in_=io["bout"], w=["borow"])
    for dc in range(ND):
        c = slice(dc * 128, (dc + 1) * 128)
        sp.dma("sync", out=xns[:, dc, :], in_=xnf_h[dc, :, :], w=[f"xns{dc}"])
        sp.dma("gpsimd", out=wo_s[:, dc, :], in_=io["woutT"][c, :],
               w=[f"wo{dc}"])
    for half in range(2):
        hc = slice(half * 512, (half + 1) * 512)
        sp.on("tensor", "matmul", r=["ones1", "borow"], w=["psY"],
              out=psY[:, hc], lhsT=ones1[:, :],
              rhs=borow[:, half * 512:(half + 1) * 512], start=True, stop=True)
    sp.on("vector", "tensor_copy", r=["psY"], w=["BO32"],
          out=BO32[:, :], in_=psY[:, :])

    def load_w(which):
        wv = io["wqkvT"][:, which * D:(which + 1) * D]
        for dc in range(ND):
            sp.dma("gpsimd", out=wp_s[:, dc, :],
                   in_=wv[dc * 128:(dc + 1) * 128, :], w=[f"wp{dc}"])
        boff = which * D
        for half in range(2):
            hc = slice(half * 512, (half + 1) * 512)
            sp.on("tensor", "matmul", r=["ones1", "brow"], w=["psY"],
                  out=psY[:, hc], lhsT=ones1[:, :],
                  rhs=brow[:, boff + half * 512:boff + (half + 1) * 512],
                  start=True, stop=True)
        sp.on("vector", "tensor_copy", r=["psY"], w=["BB32"],
              out=BB32[:, :], in_=psY[:, :])

    def project(tt):
        for half in range(2):
            hc = slice(half * 512, (half + 1) * 512)
            for dc in range(ND):
                sp.on("tensor", "matmul", r=[f"xns{dc}", f"wp{dc}"], w=["psK"],
                      out=psK[:, hc],
                      lhsT=xns[:, dc, tt * 128:(tt + 1) * 128],
                      rhs=wp_s[:, dc, hc],
                      start=(dc == 0), stop=(dc == ND - 1))
        sp.on("vector", "tensor_tensor", r=["psK", "BB32"], w=["Q32"],
              out=Q32[:, :], in0=psK[:, :], in1=BB32[:, :], op=ALU.add)

    def rope(tt):
        qv = Q32[:, :].rearrange("p (h i two) -> p h i two", h=H, two=2)
        sv = QS32[:, :].rearrange("p (h i two) -> p h i two", h=H, two=2)
        sp.on("vector", "tensor_copy", r=["Q32"], w=["QS32"],
              out=sv[:, :, :, 0:1], in_=qv[:, :, :, 1:2])
        sp.on("vector", "tensor_copy", r=["Q32"], w=["QS32"],
              out=sv[:, :, :, 1:2], in_=qv[:, :, :, 0:1])
        Qh = Q32[:, :].rearrange("p (h j) -> p h j", h=H)
        Sh = QS32[:, :].rearrange("p (h j) -> p h j", h=H)
        Rh = R1[:, :].rearrange("p (h j) -> p h j", h=H)
        for h in range(H):
            sp.on("vector", "tensor_tensor", r=["Q32", "cosd"], w=["R1"],
                  out=Rh[:, h, :], in0=Qh[:, h, :], in1=cosd[:, tt, :],
                  op=ALU.mult)
            sp.on("vector", "tensor_tensor", r=["QS32", "sind"], w=["QS32"],
                  out=Sh[:, h, :], in0=Sh[:, h, :], in1=sind[:, tt, :],
                  op=ALU.mult)
            sp.on("vector", "tensor_tensor", r=["R1", "QS32"], w=["Q32"],
                  out=Qh[:, h, :], in0=Rh[:, h, :], in1=Sh[:, h, :],
                  op=ALU.add)

    load_w(1)
    for tt in range(NT):
        project(tt)
        rope(tt)
        sp.on("vector", "tensor_copy", r=["Q32"], w=["QB"],
              out=QB[:, :], in_=Q32[:, :])
        for dc in range(ND):
            c = slice(dc * 128, (dc + 1) * 128)
            sp.on("tensor", "transpose", r=["QB", "idb"], w=["psT1"],
                  out=psT[:, c], in_=QB[:, c], identity=idb[:, :])
        sp.on("vector", "tensor_copy", r=["psT1"], w=["kf"],
              out=kf[:, :, tt * 128:(tt + 1) * 128],
              in_=psT[:, :].rearrange("p (c t) -> p c t", t=128))
    load_w(2)
    for tt in range(NT):
        project(tt)
        sp.on("vector", "tensor_copy", r=["Q32"], w=["vg"],
              out=vg[:, tt, :], in_=Q32[:, :])
    load_w(0)
    for qt in range(NT):
        tr = slice(qt * 128, (qt + 1) * 128)
        project(qt)
        rope(qt)
        sp.on("vector", "tensor_copy", r=["Q32"], w=["QB"],
              out=QB[:, :], in_=Q32[:, :])
        for dc in range(ND):
            c = slice(dc * 128, (dc + 1) * 128)
            sp.on("tensor", "transpose", r=["QB", "idb"], w=["psT1"],
                  out=psT[:, c], in_=QB[:, c], identity=idb[:, :])
        sp.on("vector", "tensor_copy", r=["psT1"], w=["qft"],
              out=qft[:, :], in_=psT[:, :])
        qfv = qft[:, :].rearrange("p (c t) -> p c t", t=128)
        ns = qt + 1
        nk = ns * 128
        for h in range(H):
            po = (h % 2) * 64
            lq = qfv[po:po + 64, h // 2, :]
            c0 = 0
            while c0 < nk:
                cw = min(512, nk - c0)
                sp.on("tensor", "matmul", r=["qft", "kf"], w=["psK"],
                      out=psS[:, 0:cw],
                      lhsT=lq, rhs=kf[po:po + 64, h // 2, c0:c0 + cw],
                      start=True, stop=True, skip_group_check=True)
                sp.on("vector", "tensor_scalar", r=["psK"], w=["S32"],
                      out=S32[:, c0:c0 + cw], in0=psS[:, 0:cw],
                      scalar1=0.125, scalar2=None, op0=ALU.mult)
                c0 += cw
            sp.on("gpsimd", "tensor_tensor", r=["S32", "tri"], w=["S32"],
                  out=S32[:, qt * 128:(qt + 1) * 128],
                  in0=S32[:, qt * 128:(qt + 1) * 128], in1=tri[:, :],
                  op=ALU.add)
            sp.on("vector", "reduce_max", r=["S32"], w=["st"],
                  out=st[:, 0:1], in_=S32[:, 0:nk], axis=AX.X, negate=True)
            sp.on("scalar", "activation", r=["S32", "st"], w=["Pbf", "st"],
                  out=Pbf[:, 0:nk], in_=S32[:, 0:nk],
                  func=AF.Exp, bias=st[:, 0:1], accum_out=st[:, 1:2])
            sp.on("vector", "reciprocal", r=["st"], w=["st"],
                  out=st[:, 2:3], in_=st[:, 1:2])
            for sb in range(0, ns, 8):
                se = min(sb + 8, ns)
                for stt in range(sb, se):
                    sp.on("tensor", "transpose", r=["Pbf", "idb"], w=["psT1"],
                          out=psT[:, (stt - sb) * 128:(stt - sb + 1) * 128],
                          in_=Pbf[:, stt * 128:(stt + 1) * 128],
                          identity=idb[:, :])
                sp.on("vector", "tensor_copy", r=["psT1"], w=["PTb"],
                      out=PTb[:, 0:(se - sb) * 128],
                      in_=psT[:, 0:(se - sb) * 128])
                for stt in range(sb, se):
                    sp.on("tensor", "matmul", r=["PTb", "vg"], w=["psO1"],
                          out=psO[:, :],
                          lhsT=PTb[:, (stt - sb) * 128:(stt - sb + 1) * 128],
                          rhs=vg[:, stt, h * 64:(h + 1) * 64],
                          start=(stt == 0), stop=(stt == ns - 1),
                          skip_group_check=True)
            sp.on("vector", "tensor_scalar", r=["psO1", "st"], w=["AO32"],
                  out=AO32[:, h * 64:(h + 1) * 64], in0=psO[:, :],
                  scalar1=st[:, 2:3], scalar2=None, op0=ALU.mult)
        sp.on("vector", "tensor_copy", r=["AO32"], w=["AObf"],
              out=AObf[:, :], in_=AO32[:, :])
        for dc in range(ND):
            c = slice(dc * 128, (dc + 1) * 128)
            sp.on("tensor", "transpose", r=["AObf", "idb"], w=["psT1"],
                  out=psT[:, c], in_=AObf[:, c], identity=idb[:, :])
        sp.on("vector", "tensor_copy", r=["psT1"], w=["Af"],
              out=Af[:, :], in_=psT[:, :])
        for half in range(2):
            hc = slice(half * 512, (half + 1) * 512)
            for dc in range(ND):
                sp.on("tensor", "matmul", r=["Af", f"wo{dc}"], w=["psY"],
                      out=psY[:, hc],
                      lhsT=Af[:, dc * 128:(dc + 1) * 128],
                      rhs=wo_s[:, dc, hc],
                      start=(dc == 0), stop=(dc == ND - 1))
        sp.on("vector", "tensor_tensor", r=["psY", "BO32"], w=["OUT32"],
              out=OUT32[:, :], in0=psY[:, :], in1=BO32[:, :], op=ALU.add)
        sp.dma("gpsimd", out=a1_h[tr, :], in_=OUT32[:, :], r=["OUT32"])


def phase_ffn1(nc, sp, ar, ps, io):
    xnf_h, gl_h, y_h = io["xnf"], io["glog"], io["y"]
    xns = ar.t([128, ND, T], BF16)
    facc = ar.t([128, NT, D], F32)
    ew_s = ar.t([128, ND, D], BF16)
    ebrow = ar.t([1, D], F32)
    EB = ar.t([128, D], F32)
    ones1 = ar.t([1, 128], F32)
    idx8 = ar.t([128, E], F32)
    l8 = ar.t([128, E], F32)
    p8 = ar.t([128, E], F32)
    r8a = ar.t([128, E], F32)
    r8b = ar.t([128, E], F32)
    r8c = ar.t([128, E], F32)
    COMB = ar.t([128, NT, E], F32)
    st = ar.t([128, 6], F32)
    Y32 = ar.t([128, D], F32)
    XA = ar.t([128, D], F32)
    XB = ar.t([128, D], F32)
    psY = ps.PB

    sp.on("vector", "memset", w=["ones1"], ap=ones1[:, :], constant=1.0)
    sp.on("vector", "memset", w=["facc"],
          ap=facc[:, :, :].rearrange("p a b -> p (a b)"), constant=0.0)
    sp.dma("sync", out=idx8[:, :], in_=io["idx8"], w=["idx8"])
    for dc in range(ND):
        sp.dma("sync", out=xns[:, dc, :], in_=xnf_h[dc, :, :], w=[f"xns{dc}"])
    for tt in range(NT):
        r = slice(tt * 128, (tt + 1) * 128)
        RT = ["l8", "p8", "r8a", "r8b", "r8c", "st", "idx8"]
        sp.dma("sync", out=l8[:, :], in_=gl_h[r, :], r=RT, w=["l8"])
        sp.on("vector", "tensor_scalar", r=RT, w=["l8"],
              out=l8[:, :], in0=l8[:, :],
              scalar1=G1EPS, scalar2=None, op0=ALU.max)
        sp.on("vector", "reduce_max", r=RT, w=["st"],
              out=st[:, 0:1], in_=l8[:, :], axis=AX.X, negate=True)
        sp.on("scalar", "activation", r=RT, w=["p8", "st"],
              out=p8[:, :], in_=l8[:, :],
              func=AF.Exp, bias=st[:, 0:1], accum_out=st[:, 1:2])
        sp.on("vector", "reciprocal", r=RT, w=["st"],
              out=st[:, 2:3], in_=st[:, 1:2])
        sp.on("vector", "tensor_scalar", r=RT, w=["p8"],
              out=p8[:, :], in0=p8[:, :],
              scalar1=st[:, 2:3], scalar2=None, op0=ALU.mult)
        sp.on("vector", "reduce_max", r=RT, w=["st"],
              out=st[:, 3:4], in_=p8[:, :], axis=AX.X, negate=True)
        sp.on("vector", "tensor_scalar", r=RT, w=["r8a"],
              out=r8a[:, :], in0=p8[:, :],
              scalar1=st[:, 3:4], scalar2=0.0, op0=ALU.add, op1=ALU.is_equal)
        sp.on("vector", "tensor_tensor", r=RT, w=["r8b"],
              out=r8b[:, :], in0=r8a[:, :], in1=idx8[:, :], op=ALU.mult)
        sp.on("vector", "scalar_tensor_tensor", r=RT, w=["r8b"],
              out=r8b[:, :], in0=r8a[:, :],
              scalar=-8.0, in1=r8b[:, :], op0=ALU.mult, op1=ALU.add)
        sp.on("vector", "tensor_scalar", r=RT, w=["r8b"],
              out=r8b[:, :], in0=r8b[:, :],
              scalar1=8.0, scalar2=None, op0=ALU.add)
        sp.on("vector", "tensor_reduce", r=RT, w=["st"],
              out=st[:, 4:5], in_=r8b[:, :], axis=AX.X, op=ALU.min)
        sp.on("vector", "tensor_scalar", r=RT, w=["r8a"],
              out=r8a[:, :], in0=idx8[:, :],
              scalar1=st[:, 4:5], scalar2=None, op0=ALU.is_equal)
        sp.on("vector", "scalar_tensor_tensor", r=RT, w=["r8b"],
              out=r8b[:, :], in0=r8a[:, :],
              scalar=-1e30, in1=p8[:, :], op0=ALU.mult, op1=ALU.add)
        sp.on("vector", "reduce_max", r=RT, w=["st"],
              out=st[:, 3:4], in_=r8b[:, :], axis=AX.X, negate=True)
        sp.on("vector", "tensor_scalar", r=RT, w=["r8c"],
              out=r8c[:, :], in0=r8b[:, :],
              scalar1=st[:, 3:4], scalar2=0.0, op0=ALU.add, op1=ALU.is_equal)
        sp.on("vector", "tensor_tensor", r=RT, w=["r8b"],
              out=r8b[:, :], in0=r8c[:, :], in1=idx8[:, :], op=ALU.mult)
        sp.on("vector", "scalar_tensor_tensor", r=RT, w=["r8b"],
              out=r8b[:, :], in0=r8c[:, :],
              scalar=-8.0, in1=r8b[:, :], op0=ALU.mult, op1=ALU.add)
        sp.on("vector", "tensor_scalar", r=RT, w=["r8b"],
              out=r8b[:, :], in0=r8b[:, :],
              scalar1=8.0, scalar2=None, op0=ALU.add)
        sp.on("vector", "tensor_reduce", r=RT, w=["st"],
              out=st[:, 4:5], in_=r8b[:, :], axis=AX.X, op=ALU.min)
        sp.on("vector", "tensor_scalar", r=RT, w=["r8c"],
              out=r8c[:, :], in0=idx8[:, :],
              scalar1=st[:, 4:5], scalar2=None, op0=ALU.is_equal)
        sp.on("vector", "tensor_tensor", r=RT, w=["r8a"],
              out=r8a[:, :], in0=r8a[:, :], in1=r8c[:, :], op=ALU.add)
        sp.on("vector", "tensor_scalar", r=RT, w=["r8b"],
              out=r8b[:, :], in0=p8[:, :],
              scalar1=WTHR, scalar2=None, op0=ALU.max)
        sp.on("vector", "tensor_tensor", r=RT, w=["r8b"],
              out=r8b[:, :], in0=r8b[:, :], in1=r8a[:, :], op=ALU.mult)
        sp.on("vector", "reduce_sum", r=RT, w=["st"],
              out=st[:, 5:6], in_=r8b[:, :], axis=AX.X)
        sp.on("vector", "reciprocal", r=RT, w=["st"],
              out=st[:, 3:4], in_=st[:, 5:6])
        sp.on("vector", "tensor_scalar", r=RT, w=["COMB"],
              out=COMB[:, tt, :], in0=r8b[:, :],
              scalar1=st[:, 3:4], scalar2=0.5, op0=ALU.mult, op1=ALU.mult)
    for e in range(E):
        ewv = io["ew"][e].rearrange("(c p) f -> p c f", p=128)
        for dc in range(ND):
            sp.dma("gpsimd", out=ew_s[:, dc, :], in_=ewv[:, dc, :],
                   w=[f"ew{dc}"])
        sp.dma("sync", out=ebrow[:, :], in_=io["eb"][e:e + 1, :], w=["ebrow"])
        for half in range(2):
            hc = slice(half * 512, (half + 1) * 512)
            sp.on("tensor", "matmul", r=["ones1", "ebrow"], w=["psY"],
                  out=psY[:, hc], lhsT=ones1[:, :],
                  rhs=ebrow[:, half * 512:(half + 1) * 512], start=True,
                  stop=True)
        sp.on("vector", "tensor_copy", r=["psY"], w=["EB"],
              out=EB[:, :], in_=psY[:, :])
        for tt in range(NT):
            for half in range(2):
                hc = slice(half * 512, (half + 1) * 512)
                for dc in range(ND):
                    sp.on("tensor", "matmul", r=[f"xns{dc}", f"ew{dc}"],
                          w=["psY"], out=psY[:, hc],
                          lhsT=xns[:, dc, tt * 128:(tt + 1) * 128],
                          rhs=ew_s[:, dc, hc],
                          start=(dc == 0), stop=(dc == ND - 1))
            sp.on("vector", "tensor_tensor", r=["psY", "EB"], w=["Y32"],
                  out=Y32[:, :], in0=psY[:, :], in1=EB[:, :], op=ALU.add)
            sp.on("vector", "scalar_tensor_tensor",
                  r=["Y32", "COMB", "facc"], w=["facc"],
                  out=facc[:, tt, :], in0=Y32[:, :],
                  scalar=COMB[:, tt, e:e + 1],
                  in1=facc[:, tt, :], op0=ALU.mult, op1=ALU.add)
    for tt in range(NT):
        r = slice(tt * 128, (tt + 1) * 128)
        sp.dma("sync", out=XA[:, :], in_=io["x1"][r, :], w=["XA"])
        sp.dma("sync", out=XB[:, :], in_=io["a1"][r, :], w=["XB"])
        sp.on("vector", "tensor_tensor", r=["XA", "XB"], w=["XA"],
              out=XA[:, :], in0=XA[:, :], in1=XB[:, :], op=ALU.add)
        sp.on("vector", "tensor_tensor", r=["XA", "facc"], w=["XA"],
              out=XA[:, :], in0=XA[:, :], in1=facc[:, tt, :], op=ALU.add)
        sp.dma("gpsimd", out=y_h[r, :], in_=XA[:, :], r=["XA"])


def build_fused(serial=False):
    nc = _bass()
    dt = nc.dram_tensor

    def din(name, shape, dtype=F32):
        return dt(name, shape, dtype, kind="ExternalInput").ap()

    def dint(name, shape, dtype=F32):
        return dt(name, shape, dtype, kind="Internal").ap()

    io = {}
    io["x"] = din("x", [T, D])
    io["idb"] = din("idb", [128, 128], BF16)
    io["tri"] = din("tri", [128, 128])
    io["cosd"] = din("cosd", [128, NT, 64])
    io["sind"] = din("sind", [128, NT, 64])
    io["idx8"] = din("idx8", [128, E])
    io["gw0"] = din("gw0", [D, E])
    io["gb0"] = din("gb0", [1, E])
    io["wqT"] = din("wqT", [D, D])
    io["wkT"] = din("wkT", [D, D])
    io["wvT"] = din("wvT", [D, D])
    io["woT"] = din("woT", [D, D])
    io["bq"] = din("bq", [1, D])
    io["bk"] = din("bk", [1, D])
    io["bv"] = din("bv", [1, D])
    io["bo"] = din("bo", [1, D])
    io["w1"] = din("w1", [E, D, F0])
    io["b1"] = din("b1", [E, F0])
    io["w2"] = din("w2", [E, F0, D])
    io["b2"] = din("b2", [E, D])
    io["gw1"] = din("gw1", [D, E])
    io["gb1"] = din("gb1", [1, E])
    io["wqkvT"] = din("wqkvT", [D, 3 * D])
    io["bqkv"] = din("bqkv", [1, 3 * D])
    io["woutT"] = din("woutT", [D, D])
    io["bout"] = din("bout", [1, D])
    io["ew"] = din("ew", [E, D, D])
    io["eb"] = din("eb", [E, D])
    io["y"] = dt("y", [T, D], F32, kind="ExternalOutput").ap()
    io["xnf0"] = dint("xnf0", [ND, 128, T], BF16)
    io["glog0"] = dint("glog0", [T, E])
    io["a0"] = dint("a0", [T, D])
    io["f0"] = dint("f0", [T, D])
    io["x1"] = dint("x1", [T, D])
    io["xnf1"] = dint("xnf1", [ND, 128, T], BF16)
    io["glog1"] = dint("glog1", [T, E])
    io["a1"] = dint("a1", [T, D])

    sp = SP(serial=serial)
    es = contextlib.ExitStack()
    with es:
        big = es.enter_context(nc.sbuf_tensor("BIG", [128, ARENA], U8))
        ps = Pool(nc, es)

        def sub(**over):
            d = dict(io)
            d.update(over)
            return d

        phase_ln(nc, sp, Arena(big), ps,
                 sub(gw=io["gw0"], gb=io["gb0"], xnf=io["xnf0"],
                     glog=io["glog0"]), residual=False)
        sp.barrier()
        phase_attn0(nc, sp, Arena(big), ps, sub(xnf=io["xnf0"]))
        sp.barrier()
        phase_ffn0(nc, sp, Arena(big), ps,
                   sub(xnf=io["xnf0"], glog=io["glog0"]))
        sp.barrier()
        phase_ln(nc, sp, Arena(big), ps,
                 sub(gw=io["gw1"], gb=io["gb1"], xnf=io["xnf1"],
                     glog=io["glog1"]), residual=True)
        sp.barrier()
        phase_attn1(nc, sp, Arena(big), ps, sub(xnf=io["xnf1"]))
        sp.barrier()
        phase_ffn1(nc, sp, Arena(big), ps,
                   sub(xnf=io["xnf1"], glog=io["glog1"]))
        emit(nc, sp)
    return nc


def _consts():
    idb = np.eye(128, dtype=BF)
    tri = np.where(np.arange(128)[None, :] <= np.arange(128)[:, None],
                   0.0, -1e30).astype(np.float32)
    inv = (10000.0 ** (-np.arange(0, DH, 2, dtype=np.float64) / DH))
    t = np.arange(T, dtype=np.float64)
    th = t[:, None] * inv[None, :]
    cosd = np.repeat(np.cos(th), 2, axis=1)
    sin2 = np.sin(th)
    sind = np.empty((T, DH))
    sind[:, 0::2] = -sin2
    sind[:, 1::2] = sin2
    cosd = cosd.reshape(NT, 128, DH).transpose(1, 0, 2).astype(np.float32).copy()
    sind = sind.reshape(NT, 128, DH).transpose(1, 0, 2).astype(np.float32).copy()
    idx8 = np.broadcast_to(np.arange(E, dtype=np.float32), (128, E)).copy()
    return idb, tri, cosd, sind, idx8


def _run(nc, maps):
    res = run_bass_kernel_spmd(nc, maps, core_ids=list(range(8)))
    return res.results


_cache = {}
_SERIAL = False


def kernel(**ins):
    idb, tri, cosd, sind, idx8 = _consts()
    cc = np.ascontiguousarray
    x = cc(np.asarray(ins["L0_attn_x"], dtype=np.float32))
    if "fused" not in _cache:
        _cache["fused"] = build_fused(serial=_SERIAL)
    nc = _cache["fused"]
    m = dict(
        idb=idb, tri=tri, cosd=cosd, sind=sind, idx8=idx8,
        gw0=cc(ins["L0_ffn_switch_w"]),
        gb0=cc(ins["L0_ffn_switch_b"]).reshape(1, E),
        wqT=cc(ins["L0_attn_Wq"].T), wkT=cc(ins["L0_attn_Wk"].T),
        wvT=cc(ins["L0_attn_Wv"].T), woT=cc(ins["L0_attn_Wo"].T),
        bq=cc(ins["L0_attn_bq"]).reshape(1, D),
        bk=cc(ins["L0_attn_bk"]).reshape(1, D),
        bv=cc(ins["L0_attn_bv"]).reshape(1, D),
        bo=cc(ins["L0_attn_bo"]).reshape(1, D),
        w1=cc(ins["L0_ffn_w1"]), b1=cc(ins["L0_ffn_b1"]),
        w2=cc(ins["L0_ffn_w2"]), b2=cc(ins["L0_ffn_b2"]),
        gw1=cc(ins["L1_ffn_gate_W"]),
        gb1=cc(ins["L1_ffn_gate_b"]).reshape(1, E),
        wqkvT=cc(ins["L1_attn_w_qkv"].T),
        bqkv=cc(ins["L1_attn_b_qkv"]).reshape(1, 3 * D),
        woutT=cc(ins["L1_attn_w_out"].T),
        bout=cc(ins["L1_attn_b_out"]).reshape(1, D),
        ew=cc(ins["L1_ffn_expert_W"]), eb=cc(ins["L1_ffn_expert_b"]),
    )
    res = _run(nc, [dict(m, x=x[c]) for c in range(8)])
    return np.stack([res[c]["y"] for c in range(8)], axis=0).astype(np.float32)


# version 11
# speedup vs baseline: 1.1867x; 1.1867x over previous
import contextlib
import numpy as np
import ml_dtypes
import concourse.bass as bass
import concourse.mybir as mybir
from concourse.bass_utils import run_bass_kernel_spmd

F32 = mybir.dt.float32
BF16 = mybir.dt.bfloat16
U8 = mybir.dt.uint8
AF = mybir.ActivationFunctionType
ALU = mybir.AluOpType
AX = mybir.AxisListType
BF = ml_dtypes.bfloat16

T, D = 2048, 1024
NT, ND = 16, 8
H, DH = 16, 64
E = 8
F0 = 4096
NFC = F0 // 128
LNEPS = 1e-5
A0EPS = 1e-6
G1EPS = 1e-4
WTHR = 0.1
GC0 = 0.044715
GC1 = 0.7978845608028654
ARENA = 207 * 1024


ENGINES = ("sync", "gpsimd", "vector", "scalar", "tensor")


NSLOT = 8


class _Step:
    __slots__ = ("eng", "meth", "kw", "inc", "cnt", "deps", "bar", "skey")


class SP:
    def __init__(self, serial=False):
        self.steps = []
        self.nq = {}
        self.last_w = {}
        self.readers = {}
        self.bar_last = {}
        self.last_of = {}
        self.last_slot = {}
        self.serial = serial

    def barrier(self):
        self.bar_last = dict(self.last_of)
        self.last_w = {}
        self.readers = {}

    def on(self, ek, meth, r=(), w=(), **kw):
        st = _Step()
        st.eng, st.meth, st.kw = ek, meth, kw
        st.deps = set()
        if meth == "dma_start":
            st.inc = 16
            n = self.nq.get(ek, 0)
            self.nq[ek] = n + 1
            st.skey = f"{ek}:{n % NSLOT}"
            prev = self.last_slot.get(st.skey)
            if prev is not None:
                st.deps.add(prev)
            self.last_slot[st.skey] = st
        else:
            st.inc = 1
            st.skey = ek
        st.deps.update(self.bar_last.values())
        for tok in r:
            lw = self.last_w.get(tok)
            if lw is not None:
                st.deps.add(lw)
            self.readers.setdefault(tok, []).append(st)
        for tok in w:
            lw = self.last_w.get(tok)
            if lw is not None:
                st.deps.add(lw)
            for rd in self.readers.get(tok, ()):
                st.deps.add(rd)
            self.readers[tok] = []
            self.last_w[tok] = st
        if self.serial and self.steps:
            st.deps.add(self.steps[-1])
        st.deps.discard(st)
        self.last_of[st.skey] = st
        self.steps.append(st)

    def dma(self, q, out, in_, r=(), w=(), **kw):
        self.on(q, "dma_start", r=r, w=w, out=out, in_=in_, **kw)


def emit(nc, sp):
    needed = set(sp.last_of.values())
    for st in sp.steps:
        needed.update(st.deps)
    cnt = {}
    for st in sp.steps:
        if st in needed:
            cnt[st.skey] = cnt.get(st.skey, 0) + st.inc
            st.cnt = cnt[st.skey]
        else:
            st.cnt = 0
    skeys = sorted(cnt)
    for k in skeys:
        assert cnt[k] < 65000, (k, cnt[k])
    es = contextlib.ExitStack()
    with es:
        sems = {k: es.enter_context(nc.semaphore("s_" + k.replace(":", "_")))
                for k in skeys}
        blk = es.enter_context(nc.Block("main"))

        def make(ek, final=False):
            def run(eng):
                waited = {k: 0 for k in skeys}
                for st in sp.steps:
                    if st.eng != ek:
                        continue
                    need = {}
                    for d in st.deps:
                        assert d.cnt > 0
                        if d.cnt > need.get(d.skey, 0):
                            need[d.skey] = d.cnt
                    for k2, v in sorted(need.items()):
                        if v > waited[k2]:
                            eng.wait_ge(sems[k2], v)
                            waited[k2] = v
                    instr = getattr(eng, st.meth)(**st.kw)
                    if st in needed:
                        instr.then_inc(sems[st.skey], st.inc)
                if final:
                    for k2 in skeys:
                        if cnt[k2] > waited[k2]:
                            eng.wait_ge(sems[k2], cnt[k2])
            return run
        blk.sync(make("sync", final=True))
        blk.gpsimd(make("gpsimd"))
        blk.vector(make("vector"))
        blk.scalar(make("scalar"))
        blk.tensor(make("tensor"))


class Arena:
    def __init__(self, big):
        self.big = big
        self.off = 0

    def t(self, shape, dtype, p0=0):
        n = int(np.prod(shape[1:]))
        nbytes = n * (2 if dtype == BF16 else 4)
        v = self.big[p0:p0 + shape[0], self.off:self.off + nbytes].bitcast(dtype)
        if len(shape) == 3:
            v = v.rearrange("p (a b) -> p a b", b=shape[2])
        elif len(shape) == 4:
            v = v.rearrange("p (a b c) -> p a b c", b=shape[2], c=shape[3])
        self.off += (nbytes + 127) & ~127
        assert self.off <= ARENA, f"arena overflow: {self.off}"
        return v


def _bass():
    return bass.Bass("TRN2", target_bir_lowering=False)


class Pool:
    def __init__(self, nc, es):
        self.PA = es.enter_context(nc.psum_tensor("PA", [128, 1024], F32))
        self.PB = es.enter_context(nc.psum_tensor("PB", [128, 1024], F32))
        self.PC = es.enter_context(nc.psum_tensor("PC", [128, 1024], F32))
        self.PT = es.enter_context(nc.psum_tensor("PT", [128, 2048], BF16))


def phase_ln(nc, sp, ar, ps, io, residual):
    x_h, gw_h, gb_h, idb_h = io["x"], io["gw"], io["gb"], io["idb"]
    xnf_h, gl_h = io["xnf"], io["glog"]
    X32 = ar.t([128, D], F32)
    A32 = ar.t([128, D], F32)
    B32 = ar.t([128, D], F32)
    SQ = ar.t([128, D], F32)
    st = ar.t([128, 32], F32)
    xh = ar.t([128, D], BF16)
    xl = ar.t([128, D], BF16)
    XNH = ar.t([128, D], BF16)
    XNL = ar.t([128, D], BF16)
    sw32 = ar.t([128, ND, E], F32)
    swh = ar.t([128, ND, E], BF16)
    swl = ar.t([128, ND, E], BF16)
    gbrow = ar.t([1, E], F32)
    ones1 = ar.t([1, 128], F32)
    idb = ar.t([128, 128], BF16)
    gl32 = ar.t([128, E], F32)
    psT, psG = ps.PT, ps.PC[:, 0:E]

    gw_v = gw_h.rearrange("(c p) e -> p c e", p=128)
    sp.on("vector", "memset", w=["ones1"], ap=ones1[:, :], constant=1.0)
    sp.dma("sync", out=idb[:, :], in_=idb_h, w=["idb"])
    sp.dma("sync", out=gbrow[:, :], in_=gb_h, w=["gbrow"])
    for dc in range(ND):
        sp.dma("sync", out=sw32[:, dc, :], in_=gw_v[:, dc, :], w=["sw32"])
    sp.on("vector", "tensor_copy", r=["sw32"], w=["swh"],
          out=swh[:, :, :], in_=sw32[:, :, :])
    sp.on("vector", "tensor_tensor", r=["sw32", "swh"], w=["swl"],
          out=swl[:, :, :], in0=sw32[:, :, :], in1=swh[:, :, :],
          op=ALU.subtract)

    for tt in range(NT):
        r = slice(tt * 128, (tt + 1) * 128)
        sp.dma("sync", out=X32[:, :], in_=x_h[r, :], w=["X32"])
        if residual:
            sp.dma("sync", out=A32[:, :], in_=io["a0"][r, :], w=["A32"])
            sp.dma("sync", out=B32[:, :], in_=io["f0"][r, :], w=["B32"])
            sp.on("vector", "tensor_tensor", r=["X32", "A32"], w=["X32"],
                  out=X32[:, :], in0=X32[:, :], in1=A32[:, :], op=ALU.add)
            sp.on("vector", "tensor_tensor", r=["X32", "B32"], w=["X32"],
                  out=X32[:, :], in0=X32[:, :], in1=B32[:, :], op=ALU.add)
            sp.dma("gpsimd", out=io["x1"][r, :], in_=X32[:, :], r=["X32"])
        sp.on("vector", "reduce_sum", r=["X32"], w=["st"],
              out=st[:, 0:1], in_=X32[:, :], axis=AX.X)
        sp.on("vector", "tensor_scalar", r=["st"], w=["st"],
              out=st[:, 1:2], in0=st[:, 0:1],
              scalar1=1.0 / D, scalar2=None, op0=ALU.mult)
        sp.on("scalar", "activation", r=["X32"], w=["SQ", "st"],
              out=SQ[:, :], in_=X32[:, :], func=AF.Square, accum_out=st[:, 2:3])
        sp.on("vector", "tensor_scalar", r=["st"], w=["st"],
              out=st[:, 3:4], in0=st[:, 2:3],
              scalar1=1.0 / D, scalar2=None, op0=ALU.mult)
        sp.on("vector", "scalar_tensor_tensor", r=["st"], w=["st"],
              out=st[:, 4:5], in0=st[:, 1:2],
              scalar=st[:, 1:2], in1=st[:, 3:4], op0=ALU.mult, op1=ALU.subtract)
        sp.on("vector", "tensor_scalar", r=["st"], w=["st"],
              out=st[:, 5:6], in0=st[:, 4:5],
              scalar1=-1.0, scalar2=LNEPS, op0=ALU.mult, op1=ALU.add)
        sp.on("scalar", "activation", r=["st"], w=["st"],
              out=st[:, 6:7], in_=st[:, 5:6], func=AF.Sqrt)
        sp.on("vector", "reciprocal", r=["st"], w=["st"],
              out=st[:, 7:8], in_=st[:, 6:7])
        sp.on("vector", "tensor_tensor", r=["st"], w=["st"],
              out=st[:, 8:9], in0=st[:, 1:2], in1=st[:, 7:8], op=ALU.mult)
        sp.on("vector", "tensor_scalar", r=["X32", "st"], w=["SQ"],
              out=SQ[:, :], in0=X32[:, :],
              scalar1=st[:, 7:8], scalar2=st[:, 8:9], op0=ALU.mult,
              op1=ALU.subtract)
        sp.on("vector", "tensor_copy", r=["SQ"], w=["xh"],
              out=xh[:, :], in_=SQ[:, :])
        sp.on("vector", "tensor_tensor", r=["SQ", "xh"], w=["xl"],
              out=xl[:, :], in0=SQ[:, :], in1=xh[:, :], op=ALU.subtract)
        for dc in range(ND):
            c = slice(dc * 128, (dc + 1) * 128)
            sp.on("tensor", "transpose", r=["xh", "idb"], w=["psT"],
                  out=psT[:, c], in_=xh[:, c], identity=idb[:, :])
        for dc in range(ND):
            sp.on("tensor", "transpose", r=["xl", "idb"], w=["psT"],
                  out=psT[:, 1024 + dc * 128:1024 + (dc + 1) * 128],
                  in_=xl[:, dc * 128:(dc + 1) * 128], identity=idb[:, :])
        sp.on("vector", "tensor_copy", r=["psT"], w=["XNH"],
              out=XNH[:, :], in_=psT[:, 0:1024])
        sp.on("vector", "tensor_copy", r=["psT"], w=["XNL"],
              out=XNL[:, :], in_=psT[:, 1024:2048])
        XNHv = XNH[:, :].rearrange("p (c t) -> p c t", t=128)
        XNLv = XNL[:, :].rearrange("p (c t) -> p c t", t=128)
        sp.on("tensor", "matmul", r=["ones1", "gbrow"], w=["psG"],
              out=psG[:, :], lhsT=ones1[:, :], rhs=gbrow[:, :],
              start=True, stop=False)
        for dc in range(ND):
            last = dc == ND - 1
            sp.on("tensor", "matmul", r=["XNH", "swh"], w=["psG"],
                  out=psG[:, :], lhsT=XNHv[:, dc, :], rhs=swh[:, dc, :],
                  start=False, stop=False)
            sp.on("tensor", "matmul", r=["XNH", "swl"], w=["psG"],
                  out=psG[:, :], lhsT=XNHv[:, dc, :], rhs=swl[:, dc, :],
                  start=False, stop=False)
            sp.on("tensor", "matmul", r=["XNL", "swh"], w=["psG"],
                  out=psG[:, :], lhsT=XNLv[:, dc, :], rhs=swh[:, dc, :],
                  start=False, stop=last)
        sp.on("vector", "tensor_copy", r=["psG"], w=["gl32"],
              out=gl32[:, :], in_=psG[:, :])
        sp.dma("gpsimd", out=gl_h[r, :], in_=gl32[:, :], r=["gl32"])
        for dc in range(ND):
            sp.dma("gpsimd", out=xnf_h[dc, :, r],
                   in_=XNH[:, dc * 128:(dc + 1) * 128], r=["XNH"])


def phase_attn0(nc, sp, ar, ps, io):
    xnf_h, a0_h, idb_h = io["xnf"], io["a0"], io["idb"]
    xns = ar.t([128, ND, T], BF16)
    wq_s = ar.t([128, ND, D], BF16)
    wk_s = ar.t([128, ND, D], BF16)
    wv_s = ar.t([128, ND, D], BF16)
    wo_s = ar.t([128, ND, D], BF16)
    brow = ar.t([1, 4 * D], F32)
    ones1 = ar.t([1, 128], F32)
    onec = ar.t([1, 1], F32)
    idb = ar.t([128, 128], BF16)
    BK32 = ar.t([128, D], F32)
    BV32 = ar.t([128, D], F32)
    BO32 = ar.t([128, D], F32)
    bqF = ar.t([128, ND], F32)
    SA = ar.t([128, D], F32)
    SB = ar.t([128, D], F32)
    SC = ar.t([128, D], F32)
    Ktm = ar.t([128, D], BF16)
    Vext = ar.t([128, H, 66], BF16)
    KVb = ar.t([128, 520], BF16)
    Qf = ar.t([128, D], BF16)
    nrm = ar.t([128, 8], F32)
    rn = ar.t([128, 8], F32)
    AO32 = ar.t([128, D], F32)
    AObf = ar.t([128, D], BF16)
    Af = ar.t([128, D], BF16)
    OUT32 = ar.t([128, D], F32)
    psA, psV, psKV = ps.PA, ps.PB, ps.PC
    psTa = ps.PT[:, 0:1024]

    sp.on("vector", "memset", w=["ones1"], ap=ones1[:, :], constant=1.0)
    sp.on("vector", "memset", w=["onec"], ap=onec[:, :], constant=1.0)
    sp.on("vector", "memset", w=["Vext"],
          ap=Vext[:, :, :].rearrange("p a b -> p (a b)"), constant=1.0)
    sp.dma("sync", out=idb[:, :], in_=idb_h, w=["idb"])
    for off, srcb in ((0, io["bq"]), (D, io["bk"]), (2 * D, io["bv"]),
                      (3 * D, io["bo"])):
        sp.dma("sync", out=brow[:, off:off + D], in_=srcb, w=["brow"])
    for dc in range(ND):
        c = slice(dc * 128, (dc + 1) * 128)
        sp.dma("sync", out=xns[:, dc, :], in_=xnf_h[dc, :, :],
               w=[f"xns{dc}"])
        sp.dma("gpsimd", out=wq_s[:, dc, :], in_=io["wqT"][c, :],
               w=[f"wq{dc}"])
        sp.dma("gpsimd", out=wk_s[:, dc, :], in_=io["wkT"][c, :],
               w=[f"wk{dc}"])
        sp.dma("gpsimd", out=wv_s[:, dc, :], in_=io["wvT"][c, :],
               w=[f"wv{dc}"])
        sp.dma("gpsimd", out=wo_s[:, dc, :], in_=io["woT"][c, :],
               w=[f"wo{dc}"])
    for dst, tok, off in ((BK32, "BK32", D), (BV32, "BV32", 2 * D),
                          (BO32, "BO32", 3 * D)):
        for half in range(2):
            hc = slice(half * 512, (half + 1) * 512)
            sp.on("tensor", "matmul", r=["ones1", "brow"], w=["psA"],
                  out=psA[:, hc], lhsT=ones1[:, :],
                  rhs=brow[:, off + half * 512:off + (half + 1) * 512],
                  start=True, stop=True)
        sp.on("vector", "tensor_copy", r=["psA"], w=[tok],
              out=dst[:, :], in_=psA[:, :])
    for dc in range(ND):
        sp.on("tensor", "matmul", r=["brow", "onec"], w=["psKV"],
              out=psKV[:, dc:dc + 1],
              lhsT=brow[:, dc * 128:(dc + 1) * 128], rhs=onec[:, :],
              start=True, stop=True)
    sp.on("vector", "tensor_copy", r=["psKV"], w=["bqF"],
          out=bqF[:, :], in_=psKV[:, 0:ND])

    for tt in range(NT):
        for half in range(2):
            hc = slice(half * 512, (half + 1) * 512)
            for dc in range(ND):
                sp.on("tensor", "matmul", r=[f"xns{dc}", f"wk{dc}"], w=["psA"],
                      out=psA[:, hc],
                      lhsT=xns[:, dc, tt * 128:(tt + 1) * 128],
                      rhs=wk_s[:, dc, hc],
                      start=(dc == 0), stop=(dc == ND - 1))
            for dc in range(ND):
                sp.on("tensor", "matmul", r=[f"xns{dc}", f"wv{dc}"], w=["psV"],
                      out=psV[:, hc],
                      lhsT=xns[:, dc, tt * 128:(tt + 1) * 128],
                      rhs=wv_s[:, dc, hc],
                      start=(dc == 0), stop=(dc == ND - 1))
        sp.on("vector", "tensor_tensor", r=["psA", "BK32"], w=["SA"],
              out=SA[:, :], in0=psA[:, :], in1=BK32[:, :], op=ALU.add)
        sp.on("vector", "tensor_scalar", r=["SA"], w=["SB"],
              out=SB[:, :], in0=SA[:, :],
              scalar1=0.0, scalar2=None, op0=ALU.max)
        sp.on("vector", "tensor_scalar", r=["SA"], w=["SC"],
              out=SC[:, :], in0=SA[:, :],
              scalar1=0.0, scalar2=None, op0=ALU.min)
        sp.on("scalar", "activation", r=["SC"], w=["SC"],
              out=SC[:, :], in_=SC[:, :], func=AF.Exp)
        sp.on("vector", "tensor_tensor", r=["SB", "SC"], w=["Ktm"],
              out=Ktm[:, :], in0=SB[:, :], in1=SC[:, :], op=ALU.add)
        sp.on("vector", "tensor_tensor", r=["psV", "BV32", "Vext"], w=["Vext"],
              out=Vext[:, :, 0:64],
              in0=psV[:, :].rearrange("p (h d) -> p h d", d=64),
              in1=BV32[:, :].rearrange("p (h d) -> p h d", d=64), op=ALU.add)
        for h in range(H):
            po = (h % 2) * 64
            pr = h // 2
            ko = (pr % 4) * 65 + (pr // 4) * 512
            sp.on("tensor", "matmul", r=["Ktm", "Vext"], w=["psKV"],
                  out=psKV[po:po + 64, ko:ko + 65],
                  lhsT=Ktm[:, h * 64:(h + 1) * 64], rhs=Vext[:, h, 0:65],
                  start=(tt == 0), stop=(tt == NT - 1), skip_group_check=True)
    sp.on("vector", "tensor_copy", r=["psKV"], w=["KVb"],
          out=KVb[:, 0:260], in_=psKV[:, 0:260])
    sp.on("vector", "tensor_copy", r=["psKV"], w=["KVb"],
          out=KVb[:, 260:520], in_=psKV[:, 512:772])

    psO = psKV

    def kvb_off(pr):
        return (pr % 4) * 65 + (pr // 4) * 260

    psOv0 = psO[:, 0:260].rearrange("p (h c) -> p h c", c=65)
    psOv1 = psO[:, 512:772].rearrange("p (h c) -> p h c", c=65)
    for tt in range(NT):
        tr = slice(tt * 128, (tt + 1) * 128)
        for pr in range(ND):
            for dc in range(ND):
                sp.on("tensor", "matmul", r=[f"wq{dc}", f"xns{dc}"], w=["psA"],
                      out=psA[:, pr * 128:(pr + 1) * 128],
                      lhsT=wq_s[:, dc, pr * 128:(pr + 1) * 128],
                      rhs=xns[:, dc, tr],
                      start=(dc == 0), stop=(dc == ND - 1))
        for pr in range(ND):
            c = slice(pr * 128, (pr + 1) * 128)
            sp.on("vector", "tensor_scalar", r=["psA", "bqF"], w=["SB"],
                  out=SB[:, c], in0=psA[:, c],
                  scalar1=bqF[:, pr:pr + 1], scalar2=0.0, op0=ALU.add,
                  op1=ALU.max)
            sp.on("vector", "tensor_scalar", r=["psA", "bqF"], w=["SC"],
                  out=SC[:, c], in0=psA[:, c],
                  scalar1=bqF[:, pr:pr + 1], scalar2=0.0, op0=ALU.add,
                  op1=ALU.min)
        sp.on("scalar", "activation", r=["SC"], w=["SC"],
              out=SC[:, :], in_=SC[:, :], func=AF.Exp)
        sp.on("vector", "tensor_tensor", r=["SB", "SC"], w=["Qf"],
              out=Qf[:, :], in0=SB[:, :], in1=SC[:, :], op=ALU.add)
        for g8 in range(2):
            for hh in range(8):
                h = g8 * 8 + hh
                po = (h % 2) * 64
                oo = (hh % 4) * 65 + (hh // 4) * 512
                sp.on("tensor", "matmul", r=["Qf", "KVb"], w=["psKV"],
                      out=psO[:, oo:oo + 65],
                      lhsT=Qf[po:po + 64, (h // 2) * 128:(h // 2) * 128 + 128],
                      rhs=KVb[po:po + 64, kvb_off(h // 2):kvb_off(h // 2) + 65],
                      start=True, stop=True, skip_group_check=True)
            sp.on("vector", "tensor_scalar", r=["psKV"], w=["nrm"],
                  out=nrm[:, 0:4], in0=psOv0[:, :, 64:65],
                  scalar1=A0EPS, scalar2=None, op0=ALU.add)
            sp.on("vector", "tensor_scalar", r=["psKV"], w=["nrm"],
                  out=nrm[:, 4:8], in0=psOv1[:, :, 64:65],
                  scalar1=A0EPS, scalar2=None, op0=ALU.add)
            sp.on("vector", "reciprocal", r=["nrm"], w=["rn"],
                  out=rn[:, :], in_=nrm[:, :])
            for hh in range(8):
                h = g8 * 8 + hh
                pv = psOv0 if hh < 4 else psOv1
                sp.on("vector", "tensor_scalar", r=["psKV", "rn"], w=["AO32"],
                      out=AO32[:, h * 64:(h + 1) * 64],
                      in0=pv[:, hh % 4, 0:64], scalar1=rn[:, hh:hh + 1],
                      scalar2=None, op0=ALU.mult)
        sp.on("vector", "tensor_copy", r=["AO32"], w=["AObf"],
              out=AObf[:, :], in_=AO32[:, :])
        for dc in range(ND):
            c = slice(dc * 128, (dc + 1) * 128)
            sp.on("tensor", "transpose", r=["AObf", "idb"], w=["psTa"],
                  out=psTa[:, c], in_=AObf[:, c], identity=idb[:, :])
        sp.on("vector", "tensor_copy", r=["psTa"], w=["Af"],
              out=Af[:, :], in_=psTa[:, :])
        for half in range(2):
            hc = slice(half * 512, (half + 1) * 512)
            for dc in range(ND):
                sp.on("tensor", "matmul", r=["Af", f"wo{dc}"], w=["psV"],
                      out=psV[:, hc],
                      lhsT=Af[:, dc * 128:(dc + 1) * 128],
                      rhs=wo_s[:, dc, hc],
                      start=(dc == 0), stop=(dc == ND - 1))
        sp.on("vector", "tensor_tensor", r=["psV", "BO32"], w=["OUT32"],
              out=OUT32[:, :], in0=psV[:, :], in1=BO32[:, :], op=ALU.add)
        sp.dma("gpsimd", out=a0_h[tr, :], in_=OUT32[:, :], r=["OUT32"])


def phase_ffn0(nc, sp, ar, ps, io):
    xnf_h, gl_h, f0_h = io["xnf"], io["glog"], io["f0"]
    w1_h, b1_h, w2_h, b2_h = io["w1"], io["b1"], io["w2"], io["b2"]
    w1_s = ar.t([128, ND, F0], BF16)
    w2_s = ar.t([128, NFC, D], BF16)
    xnt = ar.t([128, ND, 512], BF16)
    hbf = ar.t([128, NFC, 512], BF16)
    b1row = ar.t([1, D], F32)
    b2row = ar.t([1, D], F32)
    b1col = ar.t([128, NFC], F32)
    B2B = ar.t([128, D], F32)
    ones1 = ar.t([1, 128], F32)
    onec = ar.t([1, 1], F32)
    l8 = ar.t([128, E], F32)
    SEL = ar.t([128, NT, E], F32)
    RPM = ar.t([128, NT], F32)
    st = ar.t([128, 4], F32)
    GAs = (ar.t([128, 512], F32), ar.t([128, 512], F32))
    GBs = (ar.t([128, 512], F32), ar.t([128, 512], F32))
    Y32 = ar.t([128, D], F32)
    OUT32 = ar.t([128, D], F32)
    psY, psC = ps.PB, ps.PC[:, 0:NFC]
    psHs = (ps.PA[:, 0:512], ps.PA[:, 512:1024])

    sp.on("vector", "memset", w=["ones1"], ap=ones1[:, :], constant=1.0)
    sp.on("vector", "memset", w=["onec"], ap=onec[:, :], constant=1.0)
    for tt in range(NT):
        r = slice(tt * 128, (tt + 1) * 128)
        sp.dma("sync", out=l8[:, :], in_=gl_h[r, :], w=["l8"])
        sp.on("vector", "reduce_max", r=["l8"], w=["st"],
              out=st[:, 0:1], in_=l8[:, :], axis=AX.X, negate=True)
        sp.on("scalar", "activation", r=["l8", "st"], w=["SEL", "st"],
              out=SEL[:, tt, :], in_=l8[:, :],
              func=AF.Exp, bias=st[:, 0:1], accum_out=st[:, 1:2])
        sp.on("vector", "reciprocal", r=["st"], w=["RPM"],
              out=RPM[:, tt:tt + 1], in_=st[:, 1:2])
        sp.on("vector", "tensor_scalar", r=["l8", "st"], w=["SEL"],
              out=SEL[:, tt, :], in0=l8[:, :],
              scalar1=st[:, 0:1], scalar2=0.0, op0=ALU.add, op1=ALU.is_equal)
    for e in range(E):
        w1v = w1_h[e].rearrange("(c p) f -> p c f", p=128)
        w2v = w2_h[e].rearrange("(c p) f -> p c f", p=128)
        for dc in range(ND):
            sp.dma("gpsimd", out=w1_s[:, dc, :], in_=w1v[:, dc, :],
                   w=[f"w1_{dc}"])
        for fc in range(NFC):
            sp.dma("gpsimd", out=w2_s[:, fc, :], in_=w2v[:, fc, :],
                   w=[f"w2_{fc}"])
        sp.dma("sync", out=b2row[:, :], in_=b2_h[e:e + 1, :], w=["b2row"])
        for ch in range(4):
            sp.dma("sync", out=b1row[:, :],
                   in_=b1_h[e:e + 1, ch * D:(ch + 1) * D], w=["b1row"])
            for j in range(8):
                sp.on("tensor", "matmul", r=["b1row", "onec"], w=["psC"],
                      out=psC[:, ch * 8 + j:ch * 8 + j + 1],
                      lhsT=b1row[:, j * 128:(j + 1) * 128], rhs=onec[:, :],
                      start=True, stop=True)
        sp.on("vector", "tensor_copy", r=["psC"], w=["b1col"],
              out=b1col[:, :], in_=psC[:, :])
        for half in range(2):
            sp.on("tensor", "matmul", r=["ones1", "b2row"], w=["psY"],
                  out=psY[:, half * 512:(half + 1) * 512],
                  lhsT=ones1[:, :], rhs=b2row[:, half * 512:(half + 1) * 512],
                  start=True, stop=True)
        sp.on("vector", "tensor_copy", r=["psY"], w=["B2B"],
              out=B2B[:, :], in_=psY[:, :])
        for sl in range(4):
            sr = slice(sl * 512, (sl + 1) * 512)
            for dc in range(ND):
                sp.dma("sync", out=xnt[:, dc, :], in_=xnf_h[dc, :, sr],
                       w=[f"xnt{dc}"])
            for fc in range(NFC):
                par = fc % 2
                psH = psHs[par]
                htok = f"psH{par}"
                GA, GB = GAs[par], GBs[par]
                ga, gb = f"GA{par}", f"GB{par}"
                for dc in range(ND):
                    sp.on("tensor", "matmul",
                          r=[f"w1_{dc}", f"xnt{dc}"], w=[htok],
                          out=psH[:, :],
                          lhsT=w1_s[:, dc, fc * 128:(fc + 1) * 128],
                          rhs=xnt[:, dc, :],
                          start=(dc == 0), stop=(dc == ND - 1))
                sp.on("vector", "tensor_scalar", r=[htok, "b1col"], w=[ga],
                      out=GA[:, :], in0=psH[:, :],
                      scalar1=b1col[:, fc:fc + 1], scalar2=0.5, op0=ALU.add,
                      op1=ALU.mult)
                sp.on("gpsimd", "tensor_tensor", r=[ga], w=[gb],
                      out=GB[:, :], in0=GA[:, :], in1=GA[:, :], op=ALU.mult)
                sp.on("vector", "tensor_scalar", r=[gb], w=[gb],
                      out=GB[:, :], in0=GB[:, :],
                      scalar1=4.0 * GC0, scalar2=1.0, op0=ALU.mult,
                      op1=ALU.add)
                sp.on("gpsimd", "tensor_tensor", r=[gb, ga], w=[gb],
                      out=GB[:, :], in0=GB[:, :], in1=GA[:, :], op=ALU.mult)
                sp.on("scalar", "activation", r=[gb], w=[gb],
                      out=GB[:, :], in_=GB[:, :], func=AF.Tanh,
                      scale=2.0 * GC1)
                sp.on("vector", "scalar_tensor_tensor", r=[gb, ga], w=["hbf"],
                      out=hbf[:, fc, :], in0=GB[:, :], scalar=1.0,
                      in1=GA[:, :], op0=ALU.add, op1=ALU.mult)
            for q in range(4):
                tt = sl * 4 + q
                for half in range(2):
                    hc = slice(half * 512, (half + 1) * 512)
                    for fc in range(NFC):
                        sp.on("tensor", "matmul",
                              r=["hbf", f"w2_{fc}"], w=["psY"],
                              out=psY[:, hc],
                              lhsT=hbf[:, fc, q * 128:(q + 1) * 128],
                              rhs=w2_s[:, fc, hc],
                              start=(fc == 0), stop=(fc == NFC - 1))
                sp.on("vector", "tensor_tensor", r=["psY", "B2B"], w=["Y32"],
                      out=Y32[:, :], in0=psY[:, :], in1=B2B[:, :], op=ALU.add)
                tr = slice(tt * 128, (tt + 1) * 128)
                if e == 0:
                    sp.on("vector", "tensor_scalar",
                          r=["Y32", "SEL"], w=["OUT32"],
                          out=OUT32[:, :], in0=Y32[:, :],
                          scalar1=SEL[:, tt, e:e + 1],
                          scalar2=None, op0=ALU.mult)
                else:
                    sp.dma("sync", out=OUT32[:, :], in_=f0_h[tr, :],
                           r=[f"f0_{tt}"], w=["OUT32"])
                    sp.on("vector", "scalar_tensor_tensor",
                          r=["Y32", "SEL", "OUT32"], w=["OUT32"],
                          out=OUT32[:, :], in0=Y32[:, :],
                          scalar=SEL[:, tt, e:e + 1],
                          in1=OUT32[:, :], op0=ALU.mult, op1=ALU.add)
                sp.dma("gpsimd", out=f0_h[tr, :], in_=OUT32[:, :],
                       r=["OUT32"], w=[f"f0_{tt}"])
    for tt in range(NT):
        r = slice(tt * 128, (tt + 1) * 128)
        sp.dma("sync", out=OUT32[:, :], in_=f0_h[r, :],
               r=[f"f0_{tt}"], w=["OUT32"])
        sp.on("vector", "tensor_scalar", r=["OUT32", "RPM"], w=["Y32"],
              out=Y32[:, :], in0=OUT32[:, :],
              scalar1=RPM[:, tt:tt + 1], scalar2=None, op0=ALU.mult)
        sp.dma("gpsimd", out=f0_h[r, :], in_=Y32[:, :],
               r=["Y32"], w=[f"f0_{tt}"])


def phase_attn1(nc, sp, ar, ps, io):
    xnf_h, a1_h = io["xnf"], io["a1"]
    xns = ar.t([128, ND, T], BF16)
    kf = ar.t([128, ND, T], BF16)
    vg = ar.t([128, NT, D], BF16)
    wp_s = ar.t([128, ND, D], BF16)
    wo_s = ar.t([128, ND, D], BF16)
    brow = ar.t([1, 3 * D], F32)
    borow = ar.t([1, D], F32)
    ones1 = ar.t([1, 128], F32)
    idb = ar.t([128, 128], BF16)
    cosd = ar.t([128, NT, 64], F32)
    sind = ar.t([128, NT, 64], F32)
    tri = ar.t([128, 128], F32)
    BB32 = ar.t([128, D], F32)
    BO32 = ar.t([128, D], F32)
    Q32 = ar.t([128, D], F32)
    QS32 = ar.t([128, D], F32)
    R1 = ar.t([128, D], F32)
    QB = ar.t([128, D], BF16)
    qft = ar.t([128, D], BF16)
    S32 = ar.t([128, T], F32)
    Pbf = ar.t([128, T], BF16)
    PTb = ar.t([128, 1024], BF16)
    st = ar.t([128, 4], F32)
    AO32 = ar.t([128, D], F32)
    AObf = ar.t([128, D], BF16)
    Af = ar.t([128, D], BF16)
    OUT32 = ar.t([128, D], F32)
    psK, psY = ps.PA, ps.PB
    psT = ps.PT[:, 0:1024]
    psS = ps.PA[:, 0:512]
    psO = ps.PC[:, 0:64]

    sp.on("vector", "memset", w=["ones1"], ap=ones1[:, :], constant=1.0)
    sp.dma("sync", out=idb[:, :], in_=io["idb"], w=["idb"])
    sp.dma("sync", out=tri[:, :], in_=io["tri"], w=["tri"])
    sp.dma("sync", out=cosd[:, :, :], in_=io["cosd"], w=["cosd"])
    sp.dma("sync", out=sind[:, :, :], in_=io["sind"], w=["sind"])
    sp.dma("sync", out=brow[:, :], in_=io["bqkv"], w=["brow"])
    sp.dma("sync", out=borow[:, :], in_=io["bout"], w=["borow"])
    for dc in range(ND):
        c = slice(dc * 128, (dc + 1) * 128)
        sp.dma("sync", out=xns[:, dc, :], in_=xnf_h[dc, :, :], w=[f"xns{dc}"])
        sp.dma("gpsimd", out=wo_s[:, dc, :], in_=io["woutT"][c, :],
               w=[f"wo{dc}"])
    for half in range(2):
        hc = slice(half * 512, (half + 1) * 512)
        sp.on("tensor", "matmul", r=["ones1", "borow"], w=["psY"],
              out=psY[:, hc], lhsT=ones1[:, :],
              rhs=borow[:, half * 512:(half + 1) * 512], start=True, stop=True)
    sp.on("vector", "tensor_copy", r=["psY"], w=["BO32"],
          out=BO32[:, :], in_=psY[:, :])

    def load_w(which):
        wv = io["wqkvT"][:, which * D:(which + 1) * D]
        for dc in range(ND):
            sp.dma("gpsimd", out=wp_s[:, dc, :],
                   in_=wv[dc * 128:(dc + 1) * 128, :], w=[f"wp{dc}"])
        boff = which * D
        for half in range(2):
            hc = slice(half * 512, (half + 1) * 512)
            sp.on("tensor", "matmul", r=["ones1", "brow"], w=["psY"],
                  out=psY[:, hc], lhsT=ones1[:, :],
                  rhs=brow[:, boff + half * 512:boff + (half + 1) * 512],
                  start=True, stop=True)
        sp.on("vector", "tensor_copy", r=["psY"], w=["BB32"],
              out=BB32[:, :], in_=psY[:, :])

    def project(tt):
        for half in range(2):
            hc = slice(half * 512, (half + 1) * 512)
            for dc in range(ND):
                sp.on("tensor", "matmul", r=[f"xns{dc}", f"wp{dc}"], w=["psK"],
                      out=psK[:, hc],
                      lhsT=xns[:, dc, tt * 128:(tt + 1) * 128],
                      rhs=wp_s[:, dc, hc],
                      start=(dc == 0), stop=(dc == ND - 1))
        sp.on("vector", "tensor_tensor", r=["psK", "BB32"], w=["Q32"],
              out=Q32[:, :], in0=psK[:, :], in1=BB32[:, :], op=ALU.add)

    def rope(tt):
        qv = Q32[:, :].rearrange("p (h i two) -> p h i two", h=H, two=2)
        sv = QS32[:, :].rearrange("p (h i two) -> p h i two", h=H, two=2)
        sp.on("vector", "tensor_copy", r=["Q32"], w=["QS32"],
              out=sv[:, :, :, 0:1], in_=qv[:, :, :, 1:2])
        sp.on("vector", "tensor_copy", r=["Q32"], w=["QS32"],
              out=sv[:, :, :, 1:2], in_=qv[:, :, :, 0:1])
        Qh = Q32[:, :].rearrange("p (h j) -> p h j", h=H)
        Sh = QS32[:, :].rearrange("p (h j) -> p h j", h=H)
        Rh = R1[:, :].rearrange("p (h j) -> p h j", h=H)
        for h in range(H):
            sp.on("vector", "tensor_tensor", r=["Q32", "cosd"], w=["R1"],
                  out=Rh[:, h, :], in0=Qh[:, h, :], in1=cosd[:, tt, :],
                  op=ALU.mult)
            sp.on("vector", "tensor_tensor", r=["QS32", "sind"], w=["QS32"],
                  out=Sh[:, h, :], in0=Sh[:, h, :], in1=sind[:, tt, :],
                  op=ALU.mult)
            sp.on("vector", "tensor_tensor", r=["R1", "QS32"], w=["Q32"],
                  out=Qh[:, h, :], in0=Rh[:, h, :], in1=Sh[:, h, :],
                  op=ALU.add)

    load_w(1)
    for tt in range(NT):
        project(tt)
        rope(tt)
        sp.on("vector", "tensor_copy", r=["Q32"], w=["QB"],
              out=QB[:, :], in_=Q32[:, :])
        for dc in range(ND):
            c = slice(dc * 128, (dc + 1) * 128)
            sp.on("tensor", "transpose", r=["QB", "idb"], w=["psT1"],
                  out=psT[:, c], in_=QB[:, c], identity=idb[:, :])
        sp.on("vector", "tensor_copy", r=["psT1"], w=["kf"],
              out=kf[:, :, tt * 128:(tt + 1) * 128],
              in_=psT[:, :].rearrange("p (c t) -> p c t", t=128))
    load_w(2)
    for tt in range(NT):
        project(tt)
        sp.on("vector", "tensor_copy", r=["Q32"], w=["vg"],
              out=vg[:, tt, :], in_=Q32[:, :])
    load_w(0)
    for qt in range(NT):
        tr = slice(qt * 128, (qt + 1) * 128)
        project(qt)
        rope(qt)
        sp.on("vector", "tensor_copy", r=["Q32"], w=["QB"],
              out=QB[:, :], in_=Q32[:, :])
        for dc in range(ND):
            c = slice(dc * 128, (dc + 1) * 128)
            sp.on("tensor", "transpose", r=["QB", "idb"], w=["psT1"],
                  out=psT[:, c], in_=QB[:, c], identity=idb[:, :])
        sp.on("vector", "tensor_copy", r=["psT1"], w=["qft"],
              out=qft[:, :], in_=psT[:, :])
        qfv = qft[:, :].rearrange("p (c t) -> p c t", t=128)
        ns = qt + 1
        nk = ns * 128
        for h in range(H):
            po = (h % 2) * 64
            lq = qfv[po:po + 64, h // 2, :]
            c0 = 0
            while c0 < nk:
                cw = min(512, nk - c0)
                sp.on("tensor", "matmul", r=["qft", "kf"], w=["psK"],
                      out=psS[:, 0:cw],
                      lhsT=lq, rhs=kf[po:po + 64, h // 2, c0:c0 + cw],
                      start=True, stop=True, skip_group_check=True)
                sp.on("vector", "tensor_scalar", r=["psK"], w=["S32"],
                      out=S32[:, c0:c0 + cw], in0=psS[:, 0:cw],
                      scalar1=0.125, scalar2=None, op0=ALU.mult)
                c0 += cw
            sp.on("gpsimd", "tensor_tensor", r=["S32", "tri"], w=["S32"],
                  out=S32[:, qt * 128:(qt + 1) * 128],
                  in0=S32[:, qt * 128:(qt + 1) * 128], in1=tri[:, :],
                  op=ALU.add)
            sp.on("vector", "reduce_max", r=["S32"], w=["st"],
                  out=st[:, 0:1], in_=S32[:, 0:nk], axis=AX.X, negate=True)
            sp.on("scalar", "activation", r=["S32", "st"], w=["Pbf", "st"],
                  out=Pbf[:, 0:nk], in_=S32[:, 0:nk],
                  func=AF.Exp, bias=st[:, 0:1], accum_out=st[:, 1:2])
            sp.on("vector", "reciprocal", r=["st"], w=["st"],
                  out=st[:, 2:3], in_=st[:, 1:2])
            for sb in range(0, ns, 8):
                se = min(sb + 8, ns)
                for stt in range(sb, se):
                    sp.on("tensor", "transpose", r=["Pbf", "idb"], w=["psT1"],
                          out=psT[:, (stt - sb) * 128:(stt - sb + 1) * 128],
                          in_=Pbf[:, stt * 128:(stt + 1) * 128],
                          identity=idb[:, :])
                sp.on("vector", "tensor_copy", r=["psT1"], w=["PTb"],
                      out=PTb[:, 0:(se - sb) * 128],
                      in_=psT[:, 0:(se - sb) * 128])
                for stt in range(sb, se):
                    sp.on("tensor", "matmul", r=["PTb", "vg"], w=["psO1"],
                          out=psO[:, :],
                          lhsT=PTb[:, (stt - sb) * 128:(stt - sb + 1) * 128],
                          rhs=vg[:, stt, h * 64:(h + 1) * 64],
                          start=(stt == 0), stop=(stt == ns - 1),
                          skip_group_check=True)
            sp.on("vector", "tensor_scalar", r=["psO1", "st"], w=["AO32"],
                  out=AO32[:, h * 64:(h + 1) * 64], in0=psO[:, :],
                  scalar1=st[:, 2:3], scalar2=None, op0=ALU.mult)
        sp.on("vector", "tensor_copy", r=["AO32"], w=["AObf"],
              out=AObf[:, :], in_=AO32[:, :])
        for dc in range(ND):
            c = slice(dc * 128, (dc + 1) * 128)
            sp.on("tensor", "transpose", r=["AObf", "idb"], w=["psT1"],
                  out=psT[:, c], in_=AObf[:, c], identity=idb[:, :])
        sp.on("vector", "tensor_copy", r=["psT1"], w=["Af"],
              out=Af[:, :], in_=psT[:, :])
        for half in range(2):
            hc = slice(half * 512, (half + 1) * 512)
            for dc in range(ND):
                sp.on("tensor", "matmul", r=["Af", f"wo{dc}"], w=["psY"],
                      out=psY[:, hc],
                      lhsT=Af[:, dc * 128:(dc + 1) * 128],
                      rhs=wo_s[:, dc, hc],
                      start=(dc == 0), stop=(dc == ND - 1))
        sp.on("vector", "tensor_tensor", r=["psY", "BO32"], w=["OUT32"],
              out=OUT32[:, :], in0=psY[:, :], in1=BO32[:, :], op=ALU.add)
        sp.dma("gpsimd", out=a1_h[tr, :], in_=OUT32[:, :], r=["OUT32"])


def phase_ffn1(nc, sp, ar, ps, io):
    xnf_h, gl_h, y_h = io["xnf"], io["glog"], io["y"]
    xns = ar.t([128, ND, T], BF16)
    facc = ar.t([128, NT, D], F32)
    ew_s = ar.t([128, ND, D], BF16)
    ebrow = ar.t([1, D], F32)
    EB = ar.t([128, D], F32)
    ones1 = ar.t([1, 128], F32)
    idx8 = ar.t([128, E], F32)
    l8 = ar.t([128, E], F32)
    p8 = ar.t([128, E], F32)
    r8a = ar.t([128, E], F32)
    r8b = ar.t([128, E], F32)
    r8c = ar.t([128, E], F32)
    COMB = ar.t([128, NT, E], F32)
    st = ar.t([128, 6], F32)
    Y32 = ar.t([128, D], F32)
    XA = ar.t([128, D], F32)
    XB = ar.t([128, D], F32)
    psY = ps.PB

    sp.on("vector", "memset", w=["ones1"], ap=ones1[:, :], constant=1.0)
    sp.on("vector", "memset", w=["facc"],
          ap=facc[:, :, :].rearrange("p a b -> p (a b)"), constant=0.0)
    sp.dma("sync", out=idx8[:, :], in_=io["idx8"], w=["idx8"])
    for dc in range(ND):
        sp.dma("sync", out=xns[:, dc, :], in_=xnf_h[dc, :, :], w=[f"xns{dc}"])
    for tt in range(NT):
        r = slice(tt * 128, (tt + 1) * 128)
        RT = ["l8", "p8", "r8a", "r8b", "r8c", "st", "idx8"]
        sp.dma("sync", out=l8[:, :], in_=gl_h[r, :], r=RT, w=["l8"])
        sp.on("vector", "tensor_scalar", r=RT, w=["l8"],
              out=l8[:, :], in0=l8[:, :],
              scalar1=G1EPS, scalar2=None, op0=ALU.max)
        sp.on("vector", "reduce_max", r=RT, w=["st"],
              out=st[:, 0:1], in_=l8[:, :], axis=AX.X, negate=True)
        sp.on("scalar", "activation", r=RT, w=["p8", "st"],
              out=p8[:, :], in_=l8[:, :],
              func=AF.Exp, bias=st[:, 0:1], accum_out=st[:, 1:2])
        sp.on("vector", "reciprocal", r=RT, w=["st"],
              out=st[:, 2:3], in_=st[:, 1:2])
        sp.on("vector", "tensor_scalar", r=RT, w=["p8"],
              out=p8[:, :], in0=p8[:, :],
              scalar1=st[:, 2:3], scalar2=None, op0=ALU.mult)
        sp.on("vector", "reduce_max", r=RT, w=["st"],
              out=st[:, 3:4], in_=p8[:, :], axis=AX.X, negate=True)
        sp.on("vector", "tensor_scalar", r=RT, w=["r8a"],
              out=r8a[:, :], in0=p8[:, :],
              scalar1=st[:, 3:4], scalar2=0.0, op0=ALU.add, op1=ALU.is_equal)
        sp.on("vector", "tensor_tensor", r=RT, w=["r8b"],
              out=r8b[:, :], in0=r8a[:, :], in1=idx8[:, :], op=ALU.mult)
        sp.on("vector", "scalar_tensor_tensor", r=RT, w=["r8b"],
              out=r8b[:, :], in0=r8a[:, :],
              scalar=-8.0, in1=r8b[:, :], op0=ALU.mult, op1=ALU.add)
        sp.on("vector", "tensor_scalar", r=RT, w=["r8b"],
              out=r8b[:, :], in0=r8b[:, :],
              scalar1=8.0, scalar2=None, op0=ALU.add)
        sp.on("vector", "tensor_reduce", r=RT, w=["st"],
              out=st[:, 4:5], in_=r8b[:, :], axis=AX.X, op=ALU.min)
        sp.on("vector", "tensor_scalar", r=RT, w=["r8a"],
              out=r8a[:, :], in0=idx8[:, :],
              scalar1=st[:, 4:5], scalar2=None, op0=ALU.is_equal)
        sp.on("vector", "scalar_tensor_tensor", r=RT, w=["r8b"],
              out=r8b[:, :], in0=r8a[:, :],
              scalar=-1e30, in1=p8[:, :], op0=ALU.mult, op1=ALU.add)
        sp.on("vector", "reduce_max", r=RT, w=["st"],
              out=st[:, 3:4], in_=r8b[:, :], axis=AX.X, negate=True)
        sp.on("vector", "tensor_scalar", r=RT, w=["r8c"],
              out=r8c[:, :], in0=r8b[:, :],
              scalar1=st[:, 3:4], scalar2=0.0, op0=ALU.add, op1=ALU.is_equal)
        sp.on("vector", "tensor_tensor", r=RT, w=["r8b"],
              out=r8b[:, :], in0=r8c[:, :], in1=idx8[:, :], op=ALU.mult)
        sp.on("vector", "scalar_tensor_tensor", r=RT, w=["r8b"],
              out=r8b[:, :], in0=r8c[:, :],
              scalar=-8.0, in1=r8b[:, :], op0=ALU.mult, op1=ALU.add)
        sp.on("vector", "tensor_scalar", r=RT, w=["r8b"],
              out=r8b[:, :], in0=r8b[:, :],
              scalar1=8.0, scalar2=None, op0=ALU.add)
        sp.on("vector", "tensor_reduce", r=RT, w=["st"],
              out=st[:, 4:5], in_=r8b[:, :], axis=AX.X, op=ALU.min)
        sp.on("vector", "tensor_scalar", r=RT, w=["r8c"],
              out=r8c[:, :], in0=idx8[:, :],
              scalar1=st[:, 4:5], scalar2=None, op0=ALU.is_equal)
        sp.on("vector", "tensor_tensor", r=RT, w=["r8a"],
              out=r8a[:, :], in0=r8a[:, :], in1=r8c[:, :], op=ALU.add)
        sp.on("vector", "tensor_scalar", r=RT, w=["r8b"],
              out=r8b[:, :], in0=p8[:, :],
              scalar1=WTHR, scalar2=None, op0=ALU.max)
        sp.on("vector", "tensor_tensor", r=RT, w=["r8b"],
              out=r8b[:, :], in0=r8b[:, :], in1=r8a[:, :], op=ALU.mult)
        sp.on("vector", "reduce_sum", r=RT, w=["st"],
              out=st[:, 5:6], in_=r8b[:, :], axis=AX.X)
        sp.on("vector", "reciprocal", r=RT, w=["st"],
              out=st[:, 3:4], in_=st[:, 5:6])
        sp.on("vector", "tensor_scalar", r=RT, w=["COMB"],
              out=COMB[:, tt, :], in0=r8b[:, :],
              scalar1=st[:, 3:4], scalar2=0.5, op0=ALU.mult, op1=ALU.mult)
    for e in range(E):
        ewv = io["ew"][e].rearrange("(c p) f -> p c f", p=128)
        for dc in range(ND):
            sp.dma("gpsimd", out=ew_s[:, dc, :], in_=ewv[:, dc, :],
                   w=[f"ew{dc}"])
        sp.dma("sync", out=ebrow[:, :], in_=io["eb"][e:e + 1, :], w=["ebrow"])
        for half in range(2):
            hc = slice(half * 512, (half + 1) * 512)
            sp.on("tensor", "matmul", r=["ones1", "ebrow"], w=["psY"],
                  out=psY[:, hc], lhsT=ones1[:, :],
                  rhs=ebrow[:, half * 512:(half + 1) * 512], start=True,
                  stop=True)
        sp.on("vector", "tensor_copy", r=["psY"], w=["EB"],
              out=EB[:, :], in_=psY[:, :])
        for tt in range(NT):
            for half in range(2):
                hc = slice(half * 512, (half + 1) * 512)
                for dc in range(ND):
                    sp.on("tensor", "matmul", r=[f"xns{dc}", f"ew{dc}"],
                          w=["psY"], out=psY[:, hc],
                          lhsT=xns[:, dc, tt * 128:(tt + 1) * 128],
                          rhs=ew_s[:, dc, hc],
                          start=(dc == 0), stop=(dc == ND - 1))
            sp.on("vector", "tensor_tensor", r=["psY", "EB"], w=["Y32"],
                  out=Y32[:, :], in0=psY[:, :], in1=EB[:, :], op=ALU.add)
            sp.on("vector", "scalar_tensor_tensor",
                  r=["Y32", "COMB", "facc"], w=["facc"],
                  out=facc[:, tt, :], in0=Y32[:, :],
                  scalar=COMB[:, tt, e:e + 1],
                  in1=facc[:, tt, :], op0=ALU.mult, op1=ALU.add)
    for tt in range(NT):
        r = slice(tt * 128, (tt + 1) * 128)
        sp.dma("sync", out=XA[:, :], in_=io["x1"][r, :], w=["XA"])
        sp.dma("sync", out=XB[:, :], in_=io["a1"][r, :], w=["XB"])
        sp.on("vector", "tensor_tensor", r=["XA", "XB"], w=["XA"],
              out=XA[:, :], in0=XA[:, :], in1=XB[:, :], op=ALU.add)
        sp.on("vector", "tensor_tensor", r=["XA", "facc"], w=["XA"],
              out=XA[:, :], in0=XA[:, :], in1=facc[:, tt, :], op=ALU.add)
        sp.dma("gpsimd", out=y_h[r, :], in_=XA[:, :], r=["XA"])


def build_fused(serial=False):
    nc = _bass()
    dt = nc.dram_tensor

    def din(name, shape, dtype=F32):
        return dt(name, shape, dtype, kind="ExternalInput").ap()

    def dint(name, shape, dtype=F32):
        return dt(name, shape, dtype, kind="Internal").ap()

    io = {}
    io["x"] = din("x", [T, D])
    io["idb"] = din("idb", [128, 128], BF16)
    io["tri"] = din("tri", [128, 128])
    io["cosd"] = din("cosd", [128, NT, 64])
    io["sind"] = din("sind", [128, NT, 64])
    io["idx8"] = din("idx8", [128, E])
    io["gw0"] = din("gw0", [D, E])
    io["gb0"] = din("gb0", [1, E])
    io["wqT"] = din("wqT", [D, D])
    io["wkT"] = din("wkT", [D, D])
    io["wvT"] = din("wvT", [D, D])
    io["woT"] = din("woT", [D, D])
    io["bq"] = din("bq", [1, D])
    io["bk"] = din("bk", [1, D])
    io["bv"] = din("bv", [1, D])
    io["bo"] = din("bo", [1, D])
    io["w1"] = din("w1", [E, D, F0])
    io["b1"] = din("b1", [E, F0])
    io["w2"] = din("w2", [E, F0, D])
    io["b2"] = din("b2", [E, D])
    io["gw1"] = din("gw1", [D, E])
    io["gb1"] = din("gb1", [1, E])
    io["wqkvT"] = din("wqkvT", [D, 3 * D])
    io["bqkv"] = din("bqkv", [1, 3 * D])
    io["woutT"] = din("woutT", [D, D])
    io["bout"] = din("bout", [1, D])
    io["ew"] = din("ew", [E, D, D])
    io["eb"] = din("eb", [E, D])
    io["y"] = dt("y", [T, D], F32, kind="ExternalOutput").ap()
    io["xnf0"] = dint("xnf0", [ND, 128, T], BF16)
    io["glog0"] = dint("glog0", [T, E])
    io["a0"] = dint("a0", [T, D])
    io["f0"] = dint("f0", [T, D])
    io["x1"] = dint("x1", [T, D])
    io["xnf1"] = dint("xnf1", [ND, 128, T], BF16)
    io["glog1"] = dint("glog1", [T, E])
    io["a1"] = dint("a1", [T, D])

    sp = SP(serial=serial)
    es = contextlib.ExitStack()
    with es:
        big = es.enter_context(nc.sbuf_tensor("BIG", [128, ARENA], U8))
        ps = Pool(nc, es)

        def sub(**over):
            d = dict(io)
            d.update(over)
            return d

        phase_ln(nc, sp, Arena(big), ps,
                 sub(gw=io["gw0"], gb=io["gb0"], xnf=io["xnf0"],
                     glog=io["glog0"]), residual=False)
        sp.barrier()
        phase_attn0(nc, sp, Arena(big), ps, sub(xnf=io["xnf0"]))
        sp.barrier()
        phase_ffn0(nc, sp, Arena(big), ps,
                   sub(xnf=io["xnf0"], glog=io["glog0"]))
        sp.barrier()
        phase_ln(nc, sp, Arena(big), ps,
                 sub(gw=io["gw1"], gb=io["gb1"], xnf=io["xnf1"],
                     glog=io["glog1"]), residual=True)
        sp.barrier()
        phase_attn1(nc, sp, Arena(big), ps, sub(xnf=io["xnf1"]))
        sp.barrier()
        phase_ffn1(nc, sp, Arena(big), ps,
                   sub(xnf=io["xnf1"], glog=io["glog1"]))
        emit(nc, sp)
    return nc


def _consts():
    idb = np.eye(128, dtype=BF)
    tri = np.where(np.arange(128)[None, :] <= np.arange(128)[:, None],
                   0.0, -1e30).astype(np.float32)
    inv = (10000.0 ** (-np.arange(0, DH, 2, dtype=np.float64) / DH))
    t = np.arange(T, dtype=np.float64)
    th = t[:, None] * inv[None, :]
    cosd = np.repeat(np.cos(th), 2, axis=1)
    sin2 = np.sin(th)
    sind = np.empty((T, DH))
    sind[:, 0::2] = -sin2
    sind[:, 1::2] = sin2
    cosd = cosd.reshape(NT, 128, DH).transpose(1, 0, 2).astype(np.float32).copy()
    sind = sind.reshape(NT, 128, DH).transpose(1, 0, 2).astype(np.float32).copy()
    idx8 = np.broadcast_to(np.arange(E, dtype=np.float32), (128, E)).copy()
    return idb, tri, cosd, sind, idx8


def _run(nc, maps):
    res = run_bass_kernel_spmd(nc, maps, core_ids=list(range(8)))
    return res.results


_cache = {}
_SERIAL = False


def kernel(**ins):
    idb, tri, cosd, sind, idx8 = _consts()
    cc = np.ascontiguousarray
    x = cc(np.asarray(ins["L0_attn_x"], dtype=np.float32))
    if "fused" not in _cache:
        _cache["fused"] = build_fused(serial=_SERIAL)
    nc = _cache["fused"]
    m = dict(
        idb=idb, tri=tri, cosd=cosd, sind=sind, idx8=idx8,
        gw0=cc(ins["L0_ffn_switch_w"]),
        gb0=cc(ins["L0_ffn_switch_b"]).reshape(1, E),
        wqT=cc(ins["L0_attn_Wq"].T), wkT=cc(ins["L0_attn_Wk"].T),
        wvT=cc(ins["L0_attn_Wv"].T), woT=cc(ins["L0_attn_Wo"].T),
        bq=cc(ins["L0_attn_bq"]).reshape(1, D),
        bk=cc(ins["L0_attn_bk"]).reshape(1, D),
        bv=cc(ins["L0_attn_bv"]).reshape(1, D),
        bo=cc(ins["L0_attn_bo"]).reshape(1, D),
        w1=cc(ins["L0_ffn_w1"]), b1=cc(ins["L0_ffn_b1"]),
        w2=cc(ins["L0_ffn_w2"]), b2=cc(ins["L0_ffn_b2"]),
        gw1=cc(ins["L1_ffn_gate_W"]),
        gb1=cc(ins["L1_ffn_gate_b"]).reshape(1, E),
        wqkvT=cc(ins["L1_attn_w_qkv"].T),
        bqkv=cc(ins["L1_attn_b_qkv"]).reshape(1, 3 * D),
        woutT=cc(ins["L1_attn_w_out"].T),
        bout=cc(ins["L1_attn_b_out"]).reshape(1, D),
        ew=cc(ins["L1_ffn_expert_W"]), eb=cc(ins["L1_ffn_expert_b"]),
    )
    res = _run(nc, [dict(m, x=x[c]) for c in range(8)])
    return np.stack([res[c]["y"] for c in range(8)], axis=0).astype(np.float32)
